# version 21
# baseline (speedup 1.0000x reference)
import contextlib
import numpy as np

import concourse.bass as bass
import concourse.mybir as mybir
from concourse.bass_utils import run_bass_kernel_spmd

F32 = mybir.dt.float32
BF16 = mybir.dt.bfloat16
I32 = mybir.dt.int32
AF = mybir.ActivationFunctionType
ALU = mybir.AluOpType

P = 128
HI = 8
C = 1024
FHI = 32
GHI = 24
NCORES = 8
EPS = 1e-5
INV127 = np.float32(1.0 / 127.0)


class Ctr:
    def __init__(self, nc, stk, name):
        self.sem = stk.enter_context(nc.semaphore(name))
        self.n = 0


class Plan:
    ENGS = ("sync", "vector", "scalar", "tensor", "gpsimd")

    def __init__(self):
        self.ops = {e: [] for e in self.ENGS}

    def op(self, eng, builder, wait=(), inc=None):
        pairs = tuple((c.sem, v) for (c, v) in wait if v is not None)
        tgt = None
        if inc is not None:
            ctr, k = inc
            ctr.n += k
            tgt = (ctr.sem, k)
        def f(e, pairs=pairs, builder=builder, tgt=tgt):
            for s, v in pairs:
                e.wait_ge(s, v)
            ins = builder(e)
            if tgt is not None:
                ins.then_inc(tgt[0], tgt[1])
        self.ops[eng].append(f)
        return None

    def wait(self, eng, ctr, val):
        self.op(eng, lambda e: None, wait=((ctr, val),))

    def drain(self, eng):
        self.ops[eng].append(lambda e: e.drain())

    def emit(self, nc):
        with nc.Block() as block:
            @block.sync
            def _(e):
                for f in self.ops["sync"]:
                    f(e)
            @block.vector
            def _(e):
                for f in self.ops["vector"]:
                    f(e)
            @block.scalar
            def _(e):
                for f in self.ops["scalar"]:
                    f(e)
            @block.tensor
            def _(e):
                for f in self.ops["tensor"]:
                    f(e)
            @block.gpsimd
            def _(e):
                for f in self.ops["gpsimd"]:
                    f(e)


def build_nc(T=4096, TS=256, nseq=2, debug_scratch=False):
    NTT = T // TS
    NT = nseq * NTT
    FD = HI * TS
    KFD = FHI * TS
    GFD = GHI * TS

    nc = bass.Bass("TRN2", target_bir_lowering=False, debug=False)

    x_in = nc.dram_tensor("x", [NT, P, FD], F32, kind="ExternalInput").ap()
    out_d = nc.dram_tensor("out", [NT, P, FD], F32, kind="ExternalOutput").ap()
    wd_in = {}
    for nm, shp, dt in (("wkT", [C, C], F32), ("wvT", [C, C], F32),
                        ("wrT", [C, C], F32), ("woT", [C, C], F32),
                        ("fwkT", [C, 4096], F32), ("fwrT", [C, C], F32),
                        ("fwvT", [4096, C], F32), ("wgT", [C, 3072], I32),
                        ("wuT", [C, 3072], I32), ("wdT", [3072, C], I32)):
        wd_in[nm] = nc.dram_tensor(nm, shp, dt, kind="ExternalInput").ap()
    vec_in = {}
    for nm in ("ln1_g", "ln1_b", "ln2_g", "ln2_b", "td", "tf",
               "tmk", "tmv", "tmr", "ftmk", "ftmr"):
        vec_in[nm] = nc.dram_tensor(nm, [C], F32, kind="ExternalInput").ap()

    _kind = "ExternalOutput" if debug_scratch else "Internal"
    dbg_d = nc.dram_tensor("dbg", [NT, P, 4 * TS], F32, kind=_kind).ap() \
        if debug_scratch else None
    sry_s = nc.dram_tensor("sry_s", [NT, P, FD], BF16, kind=_kind).ap()
    kk_s = nc.dram_tensor("kk_s", [NT, P, KFD], BF16, kind=_kind).ap()
    r2_s = nc.dram_tensor("r2_s", [NT, P, FD], BF16, kind=_kind).ap()
    s_s = nc.dram_tensor("s_s", [NT, P, FD], F32, kind=_kind).ap()
    x3_s = nc.dram_tensor("x3_s", [NT, P, FD], F32, kind=_kind).ap()

    with contextlib.ExitStack() as g:
        def sbg(name, shape, dt=F32):
            return g.enter_context(nc.sbuf_tensor(name, shape, dt))

        ones = sbg("ones", [P, P], BF16)
        epsb = sbg("epsb", [P, 1])
        vv = {nm: sbg("v_" + nm, [P, HI]) for nm in vec_in}
        wvec = sbg("wvec", [P, HI])
        tmp8 = sbg("tmp8", [P, HI])
        BVk = sbg("BVk", [P, HI * 2], BF16); BVv = sbg("BVv", [P, HI * 2], BF16)
        BVr = sbg("BVr", [P, HI * 2], BF16)
        BFk = sbg("BFk", [P, HI * 2], BF16); BFr = sbg("BFr", [P, HI * 2], BF16)
        bias_k = sbg("bias_k", [P, HI]); ckv_ = sbg("ck", [P, HI])
        bias_v = sbg("bias_v", [P, HI]); cvv_ = sbg("cv", [P, HI])
        bias_r = sbg("bias_r", [P, HI]); crv_ = sbg("cr", [P, HI])
        bias_fk = sbg("bias_fk", [P, FHI]); cfk = sbg("cfk", [P, FHI])
        bias_fr = sbg("bias_fr", [P, HI]); cfr = sbg("cfr", [P, HI])

        PS = g.enter_context(nc.psum_tensor("PS", [P, 4096], F32))

        pl = Plan()
        c_wc = Ctr(nc, g, "c_wc")
        c_pe = Ctr(nc, g, "c_pe")
        c_sc = Ctr(nc, g, "c_sc")
        c_ve = Ctr(nc, g, "c_ve")
        c_bar = Ctr(nc, g, "c_bar")
        c_vec = Ctr(nc, g, "c_vec")
        c_gp = Ctr(nc, g, "c_gp")
        c_stg = Ctr(nc, g, "c_stg")
        c_x = [Ctr(nc, g, "c_x0"), Ctr(nc, g, "c_x1")]
        c_sry = [Ctr(nc, g, "c_sry0"), Ctr(nc, g, "c_sry1")]
        c_b2 = [Ctr(nc, g, "c_b20"), Ctr(nc, g, "c_b21")]
        c_sp_sry = Ctr(nc, g, "c_sp_sry")
        c_sp_kk = Ctr(nc, g, "c_sp_kk")
        c_sp_r2 = Ctr(nc, g, "c_sp_r2")
        c_sp_s = Ctr(nc, g, "c_sp_s")
        c_sp_x3 = Ctr(nc, g, "c_sp_x3")
        c_sp_out = Ctr(nc, g, "c_sp_out")
        DMA_CTRS = [c_vec, c_stg, c_x[0], c_x[1], c_sry[0], c_sry[1],
                    c_b2[0], c_b2[1], c_sp_sry, c_sp_kk, c_sp_r2, c_sp_s,
                    c_sp_x3, c_sp_out]

        def barrier():
            pl.op("vector", lambda e: e.memset(tmp8[:, 0:1], 0.0), inc=(c_bar, 1))
            pl.op("scalar", lambda e: e.activation(out=tmp8[:, 1:2], in_=tmp8[:, 1:2],
                                                   func=AF.Copy), inc=(c_bar, 1))
            pl.op("tensor", lambda e: e.matmul(PS[:, 0:1], ones[:, 0:P],
                                               ones[:, 0:1], start=True, stop=True),
                  wait=((c_sc, c_sc.n), (c_ve, c_ve.n)), inc=(c_bar, 1))
            pl.op("gpsimd", lambda e: e.memset(tmp8[:, 2:3], 0.0), inc=(c_bar, 1))
            tot = c_bar.n
            dma_waits = tuple((c, c.n) for c in DMA_CTRS)
            for eng in Plan.ENGS:
                pl.op(eng, lambda e: None, wait=((c_bar, tot),) + dma_waits)

        def bcast_rows(row):
            return row[:].unsqueeze(1).broadcast_to([P, HI, TS])

        def bcast_cols(col8):
            return col8[:].unsqueeze(2).broadcast_to([P, HI, TS])

        for nm in vec_in:
            pl.op("sync", lambda e, nm=nm: e.dma_start(
                out=vv[nm][:], in_=vec_in[nm].rearrange("(h p) -> p h", p=P),
                allow_slow_non_contiguous=True), inc=(c_vec, 16))
        pl.op("vector", lambda e: e.memset(ones[:], 1.0))
        pl.op("vector", lambda e: e.memset(epsb[:], EPS))
        pl.op("scalar", lambda e: e.activation(out=tmp8[:], in_=vv["td"][:],
                                               func=AF.Exp),
              wait=((c_vec, c_vec.n),), inc=(c_sc, 1))
        pl.op("vector", lambda e: e.tensor_scalar(out=wvec[:], in0=tmp8[:],
                                                  scalar1=-1.0, scalar2=None,
                                                  op0=ALU.mult),
              wait=((c_sc, c_sc.n), (c_vec, c_vec.n)))
        def mk_bv(e):
            for arr, tm, bvec in ((BVk, "tmk", "ln1_b"), (BVv, "tmv", "ln1_b"),
                                  (BVr, "tmr", "ln1_b"), (BFk, "ftmk", "ln2_b"),
                                  (BFr, "ftmr", "ln2_b")):
                av = arr.rearrange("p (h j) -> p h j", j=2)
                e.tensor_copy(out=av[:, :, 0], in_=vv[bvec][:])
                e.drain()
                e.tensor_scalar(out=tmp8[:], in0=vv[tm][:], scalar1=1.0,
                                scalar2=None, op0=ALU.subtract)
                e.drain()
                e.tensor_tensor(out=av[:, :, 1], in0=tmp8[:], in1=vv[bvec][:],
                                op=ALU.mult)
                e.drain()
            e.drain()
            return e.memset(tmp8[:, 0:1], 0.0)
        pl.op("vector", mk_bv, inc=(c_ve, 1))
        vec_ready = c_ve.n

        def stage_weights(alloc, specs):
            stF = alloc("stF", [P, 4096])
            stB = alloc("stB", [P, 4096], BF16)
            acc = alloc("acc", [P, 64 * 2])
            pe_slab = [0]
            for (dram, no, res, gain, kch, barr, noj, bdst, cdst) in specs:
                if barr is not None:
                    pl.op("vector", lambda e: e.memset(acc[:], 0.0),
                          wait=((c_pe, pe_slab[0]),), inc=(c_ve, 1))
                for hi in range(kch):
                    src = dram[hi * P:(hi + 1) * P, :]
                    dst = stF[:, 0:no] if dram.dtype == F32 else \
                        stF.bitcast(dram.dtype)[:, 0:no]
                    pl.op("sync", lambda e, dst=dst, src=src: e.dma_start(
                        out=dst, in_=src),
                        wait=((c_wc, c_wc.n), (c_pe, pe_slab[0])),
                        inc=(c_stg, 16))
                    ldn = c_stg.n
                    srcv = dst
                    if barr is not None:
                        pl.op("vector", lambda e, srcv=srcv, no=no: e.tensor_copy(
                            out=stB[:, 0:no], in_=srcv),
                            wait=((c_stg, ldn), (c_pe, pe_slab[0])), inc=(c_ve, 1))
                        ven = c_ve.n
                        for oj in range(noj):
                            poff = (oj % 8) * 512 + (oj // 8) * 2
                            pl.op("tensor", lambda e, poff=poff, oj=oj, hi=hi,
                                  barr=barr: e.matmul(
                                      PS[:, poff:poff + 2],
                                      stB[:, oj * 128:(oj + 1) * 128],
                                      barr[:, hi * 2:hi * 2 + 2],
                                      start=True, stop=True),
                                  wait=((c_ve, ven), (c_ve, vec_ready))
                                  if oj == 0 else (),
                                  inc=(c_pe, 1) if oj == noj - 1 else None)
                        pe_slab[0] = c_pe.n
                        psv = PS.rearrange("p (a b c) -> p a b c", b=256, c=2)
                        pl.op("vector", lambda e, noj=noj, psv=psv: e.tensor_tensor(
                            out=acc.rearrange("p (a b c) -> p a b c",
                                              a=min(noj, 8), c=2)[:, :, 0:max(noj // 8, 1), :],
                            in0=acc.rearrange("p (a b c) -> p a b c",
                                              a=min(noj, 8), c=2)[:, :, 0:max(noj // 8, 1), :],
                            in1=psv[:, 0:min(noj, 8), 0:max(noj // 8, 1), :],
                            op=ALU.add),
                            wait=((c_pe, pe_slab[0]),), inc=(c_ve, 1))
                        pl.drain("vector")
                        pe_slab[0] = c_pe.n
                    if gain is not None:
                        pl.op("vector", lambda e, res=res, hi=hi, no=no, gain=gain:
                              e.tensor_scalar(out=res[:, hi * no:(hi + 1) * no],
                                              in0=stF[:, 0:no],
                                              scalar1=gain[:, hi:hi + 1],
                                              scalar2=None, op0=ALU.mult),
                              wait=((c_stg, ldn), (c_ve, vec_ready)), inc=(c_wc, 1))
                    else:
                        pl.op("vector", lambda e, res=res, hi=hi, no=no, srcv=srcv:
                              e.tensor_copy(out=res[:, hi * no:(hi + 1) * no],
                                            in_=srcv),
                              wait=((c_stg, ldn),), inc=(c_wc, 1))
                if barr is not None:
                    nb = max(noj // 8, 1)
                    accv = acc.rearrange("p (a b c) -> p a b c", a=min(noj, 8), c=2)
                    def ev(e, bdst=bdst, cdst=cdst, accv=accv, noj=noj, nb=nb):
                        bv = bdst.rearrange("p (b a) -> p a b", a=min(noj, 8))
                        cvx = cdst.rearrange("p (b a) -> p a b", a=min(noj, 8))
                        e.tensor_copy(out=bv[:, :, :], in_=accv[:, :, 0:nb, 0])
                        return e.tensor_copy(out=cvx[:, :, :], in_=accv[:, :, 0:nb, 1])
                    pl.drain("vector")
                    pl.op("vector", ev, inc=(c_ve, 1))
                    pl.drain("vector")

        def _phaseA():
          with contextlib.ExitStack() as ph:
            def sb(name, shape, dt=F32):
                return ph.enter_context(nc.sbuf_tensor("A_" + name, shape, dt))

            wk_res = sb("wk", [P, HI * C], BF16)
            wv_res = sb("wvw", [P, HI * C], BF16)
            wr_res = sb("wr", [P, HI * C], BF16)
            with contextlib.ExitStack() as stg:
                def salloc(n, s, d=F32):
                    return stg.enter_context(nc.sbuf_tensor("As_" + n, s, d))
                stage_weights(salloc, [
                    (wd_in["wkT"], C, wk_res, vv["ln1_g"], HI, BVk, HI, bias_k, ckv_),
                    (wd_in["wvT"], C, wv_res, vv["ln1_g"], HI, BVv, HI, bias_v, cvv_),
                    (wd_in["wrT"], C, wr_res, vv["ln1_g"], HI, BVr, HI, bias_r, crv_)])
            wA_ready = c_ve.n
            wcA = c_wc.n

            xA = [sb("x0", [P, FD]), sb("x1", [P, FD])]
            xbf = sb("xbf", [P, FD], BF16)
            mrow = sb("mrow", [P, TS]); varrow = sb("varrow", [P, TS])
            rstd = sb("rstd", [P, TS])
            tmpA = sb("tmpA", [P, FD])
            zn = sb("zn", [P, HI * (TS + 1)], BF16)
            dmix = sb("dmix", [P, FD], BF16)
            xk = sb("xk", [P, FD], BF16)
            x2bf = xk
            xvm = sb("xvm", [P, FD], BF16)
            xr = sb("xr", [P, FD], BF16)
            PD = TS + 1
            FDP = HI * PD
            kp = [sb("k0", [P, FDP]), sb("k1", [P, FDP])]
            vb = [sb("v0", [P, FD], BF16), sb("v1", [P, FD], BF16)]
            srb = [sb("sr0", [P, FD], BF16), sb("sr1", [P, FD], BF16)]
            PPt = sb("PP", [P, FDP])
            AAt = sb("AA", [P, FDP])
            BBt = sb("BB", [P, FDP])
            e1p = sb("e1p", [P, FDP]); e2p = sb("e2p", [P, FDP])
            Pp = sb("Pp", [P, FDP], BF16)
            Buq = sb("Buq", [P, FD]); Byw = sb("Byw", [P, FD])
            wpad = sb("wpad", [P, FDP])
            sryb = sb("sry", [P, FD], BF16)

            znv = zn.rearrange("p (h t) -> p h t", h=HI)
            PPv = PPt.rearrange("p (h t) -> p h t", h=HI)
            AAv = AAt.rearrange("p (h t) -> p h t", h=HI)
            BBv = BBt.rearrange("p (h t) -> p h t", h=HI)
            e1pv = e1p.rearrange("p (h t) -> p h t", h=HI)
            e2pv = e2p.rearrange("p (h t) -> p h t", h=HI)
            Ppv = Pp.rearrange("p (h t) -> p h t", h=HI)
            wpv = wpad.rearrange("p (h t) -> p h t", h=HI)

            pl.drain("vector")
            for hi in range(HI):
                pl.op("vector", lambda e, hi=hi: e.tensor_copy(
                    out=wpad[:, hi * PD + 1:(hi + 1) * PD],
                    in_=wvec[:, hi:hi + 1].broadcast_to([P, TS])))
            pl.op("vector", lambda e: e.memset(wpv[:, :, 0:1], -1e38))
            pl.op("vector", lambda e: e.memset(e1pv[:, :, 0:1], 0.0))
            pl.op("vector", lambda e: e.memset(e2pv[:, :, 0:1], 0.0))
            pl.drain("vector")
            xAv = [a.rearrange("p (h t) -> p h t", h=HI) for a in xA]
            tmpAv = tmpA.rearrange("p (h t) -> p h t", h=HI)
            dmv = dmix.rearrange("p (h t) -> p h t", h=HI)

            PS_R = 1024
            def st_off(t):
                return (t % 2) * 512
            def k_off(oj):
                return 3072 + oj * TS if oj < 4 else 1024 + (oj - 4) * TS

            sq_d = [0] * (NT + 1); st_d = [0] * (NT + 1)
            vv_d = [0] * (NT + 1); rs_d = [0] * (NT + 1)
            xf_d = [0] * (NT + 1); mx_d = [0] * (NT + 1)
            pjr = [0] * (NT + 1); pjk = [0] * (NT + 1); pjv = [0] * (NT + 1)
            evk1 = [0] * (NT + 1)
            evs = [0] * (NT + 1); evk = [0] * (NT + 1); evv = [0] * (NT + 1)
            vy_d = [0] * (NT + 1); sp_d = [0] * (NT + 1)
            rden_sL = [0] * (NT + 1); num_gL = [0] * (NT + 1)
            d1_vL = [0] * (NT + 1)
            y_gL = [0] * (NT + 1); sry_gL = [0] * (NT + 1)
            x_ld = [0] * (NT + 2)

            def load_x(r, first=False):
                w = ((c_ve, wA_ready), (c_wc, wcA)) if first \
                    else ((c_ve, xf_d[r - 2]),)
                sl = c_x[r % 2]
                pl.op("sync", lambda e, r=r: e.dma_start(out=xA[r % 2][:],
                                                         in_=x_in[r, :, :]),
                      wait=w, inc=(sl, 16))
                x_ld[r] = (sl, sl.n)
            load_x(0, True)
            if NT > 1:
                load_x(1, True)

            for r in range(NT + 2):
                if r >= 2:
                    j2 = r - 2
                    pl.op("gpsimd", lambda e: e.tensor_tensor(
                        out=Buq[:], in0=Buq[:], in1=tmpA[:], op=ALU.mult),
                        wait=((c_sc, rden_sL[j2]),), inc=(c_gp, 1))
                    y_gL[j2] = c_gp.n
                    pl.drain("gpsimd")
                    wsry = []
                    if j2 >= 1:
                        wsry.append((c_sp_sry, sp_d[j2 - 1]))
                    pl.op("gpsimd", lambda e, j2=j2: e.tensor_tensor(
                        out=sryb[:], in0=Buq[:], in1=srb[j2 % 2][:],
                        op=ALU.mult), wait=wsry, inc=(c_gp, 1))
                    sry_gL[j2] = c_gp.n
                    pl.drain("gpsimd")
                    pl.op("gpsimd", lambda e, j2=j2: e.dma_start(
                        out=sry_s[j2, :, :], in_=sryb[:]), inc=(c_sp_sry, 16))
                    sp_d[j2] = c_sp_sry.n
                def emit_castats(t):
                    w = [x_ld[t]]
                    if t >= 1:
                        w.append((c_pe, st_d[t - 1]))
                    pl.op("scalar", lambda e, t=t: e.activation(
                        out=xbf[:], in_=xA[t % 2][:], func=AF.Identity),
                        wait=w)
                    pl.drain("scalar")
                    pl.op("scalar", lambda e: e.activation(
                        out=x2bf[:], in_=xbf[:], func=AF.Square),
                        inc=(c_sc, 1))
                    sq_d[t] = c_sc.n
                    base = st_off(t)
                    wst = [(c_sc, sq_d[t])]
                    if t >= 2:
                        wst.append((c_ve, vv_d[t - 2]))
                    if t == 0:
                        wst += [(c_ve, wA_ready), (c_wc, wcA)]
                    for hi in range(HI):
                        pl.op("tensor", lambda e, hi=hi, base=base: e.matmul(
                            PS[:, base:base + TS], ones[:, :],
                            xbf[:, hi * TS:(hi + 1) * TS], start=(hi == 0),
                            stop=(hi == HI - 1)), wait=wst if hi == 0 else ())
                    for hi in range(HI):
                        pl.op("tensor", lambda e, hi=hi, base=base: e.matmul(
                            PS[:, base + TS:base + 2 * TS], ones[:, :],
                            x2bf[:, hi * TS:(hi + 1) * TS], start=(hi == 0),
                            stop=(hi == HI - 1)),
                            inc=(c_pe, 1) if hi == HI - 1 else None)
                    st_d[t] = c_pe.n
                if r == 0:
                    emit_castats(0)
                if r < NT:
                    sstart = (r % NTT == 0)
                    xcur, xcv = xA[r % 2], xAv[r % 2]
                    pl.op("vector", lambda e, r=r: e.tensor_scalar(
                        out=mrow[:], in0=PS[:, st_off(r):st_off(r) + TS],
                        scalar1=1.0 / C, scalar2=None, op0=ALU.mult),
                        wait=((c_pe, st_d[r]),), inc=(c_ve, 1))
                    mn_v = c_ve.n
                    pl.op("scalar", lambda e: e.activation(
                        out=varrow[:], in_=mrow[:], func=AF.Square),
                        wait=((c_ve, mn_v),), inc=(c_sc, 1))
                    msq_s = c_sc.n
                    pl.op("vector", lambda e, r=r: e.scalar_tensor_tensor(
                        out=varrow[:],
                        in0=PS[:, st_off(r) + TS:st_off(r) + 2 * TS],
                        scalar=1.0 / C, in1=varrow[:], op0=ALU.mult,
                        op1=ALU.subtract), wait=((c_sc, msq_s),),
                        inc=(c_ve, 1))
                    vv_d[r] = c_ve.n
                    pl.op("scalar", lambda e: e.activation(
                        out=rstd[:], in_=varrow[:], func=AF.Ln,
                        bias=epsb[:, 0:1]), wait=((c_ve, vv_d[r]),))
                    pl.drain("scalar")
                    pl.op("scalar", lambda e: e.activation(
                        out=rstd[:], in_=rstd[:], func=AF.Exp, scale=-0.5),
                        inc=(c_sc, 1))
                    rs_d[r] = c_sc.n
                    if r + 1 < NT:
                        emit_castats(r + 1)
                    if sstart:
                        pl.op("vector", lambda e: e.memset(znv[:, :, 0:1], 0.0))
                    else:
                        pl.drain("vector")
                        pl.op("vector", lambda e: e.tensor_copy(
                            out=znv[:, :, 0:1], in_=znv[:, :, TS:TS + 1]))
                    wap = [(c_sc, rs_d[r])]
                    if r >= 2:
                        wap.append((c_gp, y_gL[r - 2]))
                    pl.op("vector", lambda e, xcv=xcv: e.tensor_tensor(
                        out=tmpAv[:, :, :], in0=xcv[:, :, :],
                        in1=bcast_rows(mrow), op=ALU.subtract),
                        wait=wap)
                    pl.drain("vector")
                    pl.op("vector", lambda e: e.tensor_tensor(
                        out=znv[:, :, 1:TS + 1], in0=tmpAv[:, :, :],
                        in1=bcast_rows(rstd), op=ALU.mult), inc=(c_ve, 1))
                    xf_d[r] = c_ve.n
                    pl.drain("vector")
                    def front_b(r=r, sstart=sstart, xcv=xcv):
                        pl.op("vector", lambda e: e.tensor_tensor(
                            out=dmv[:, :, :], in0=znv[:, :, 1:TS + 1],
                            in1=znv[:, :, 0:TS], op=ALU.subtract))
                        pl.drain("vector")
                        for dst, tm in ((xk, "tmk"), (xvm, "tmv"), (xr, "tmr")):
                            for hi in range(HI):
                                first = (dst is xk and hi == 0)
                                last = (dst is xr and hi == HI - 1)
                                wmx = []
                                if first and r + 1 < NT:
                                    wmx.append((c_pe, st_d[r + 1]))
                                if first and r >= 2:
                                    wmx.append((c_gp, num_gL[r - 2]))
                                pl.op("vector", lambda e, dst=dst, tm=tm, hi=hi:
                                      e.scalar_tensor_tensor(
                                          out=dst[:, hi * TS:(hi + 1) * TS],
                                          in0=dmix[:, hi * TS:(hi + 1) * TS],
                                          scalar=vv[tm][:, hi:hi + 1],
                                          in1=znv[:, hi, 0:TS],
                                          op0=ALU.mult, op1=ALU.add),
                                      wait=wmx,
                                      inc=(c_ve, 1) if last else None)
                        mx_d[r] = c_ve.n

                        w = [(c_ve, mx_d[r])]
                        if r >= 1:
                            w.append((c_sc, evk[r - 1]))
                        for oj in range(HI):
                            for hi in range(HI):
                                pl.op("tensor", lambda e, oj=oj, hi=hi: e.matmul(
                                    PS[:, PS_R + oj * TS:PS_R + (oj + 1) * TS],
                                    wr_res[:, hi * C + oj * 128:hi * C + oj * 128 + 128],
                                    xr[:, hi * TS:(hi + 1) * TS],
                                    start=(hi == 0), stop=(hi == HI - 1)),
                                    wait=w if (oj == 0 and hi == 0) else (),
                                    inc=(c_pe, 1) if (oj == HI - 1 and hi == HI - 1)
                                    else None)
                        pjr[r] = c_pe.n
                        vcor = None
                        if sstart:
                            psr = PS[:, PS_R:PS_R + HI * TS].rearrange(
                                "p (o t) -> p o t", o=HI)
                            pl.op("vector", lambda e, psr=psr: e.tensor_tensor(
                                out=psr[:, :, 0:1], in0=psr[:, :, 0:1],
                                in1=crv_[:].unsqueeze(2), op=ALU.add),
                                wait=((c_pe, pjr[r]),), inc=(c_ve, 1))
                            vcor = c_ve.n
                        w = [(c_ve, mx_d[r])]
                        if r >= 1:
                            w.append((c_sc, evk1[r - 1]))
                        for oj in range(4):
                            for hi in range(HI):
                                pl.op("tensor", lambda e, oj=oj, hi=hi: e.matmul(
                                    PS[:, k_off(oj):k_off(oj) + TS],
                                    wk_res[:, hi * C + oj * 128:hi * C + oj * 128 + 128],
                                    xk[:, hi * TS:(hi + 1) * TS],
                                    start=(hi == 0), stop=(hi == HI - 1)),
                                    wait=w if (oj == 0 and hi == 0) else (),
                                    inc=(c_pe, 1) if (oj == 3 and hi == HI - 1)
                                    else None)
                        pjk[r] = c_pe.n
                        return vcor

                if r == 0:
                    vcor = front_b()
                if 1 <= r <= NT:
                    j = r - 1
                    jstart = (j % NTT == 0)
                    kc, vc, src_ = kp[j % 2], vb[j % 2], srb[j % 2]
                    kcv = kc.rearrange("p (h t) -> p h t", h=HI)
                    vcw = vc.rearrange("p (h t) -> p h t", h=HI)
                    Bqv = Buq.rearrange("p (h t) -> p h t", h=HI)
                    Bwv = Byw.rearrange("p (h t) -> p h t", h=HI)
                    if jstart:
                        pl.op("vector", lambda e, kcv=kcv: e.memset(
                            kcv[:, :, 0:1], -1e38), wait=((c_sc, evk[j]),))
                        pl.op("vector", lambda e, kcv=kcv: e.tensor_tensor(
                            out=kcv[:, :, 1:2], in0=kcv[:, :, 1:2],
                            in1=ckv_[:].unsqueeze(2), op=ALU.add))
                        pl.op("vector", lambda e, vcw=vcw: e.tensor_tensor(
                            out=vcw[:, :, 0:1], in0=vcw[:, :, 0:1],
                            in1=cvv_[:].unsqueeze(2), op=ALU.add),
                            wait=((c_sc, evv[j]),))
                    else:
                        pl.drain("vector")
                        pl.op("vector", lambda e, kcv=kcv: e.tensor_copy(
                            out=kcv[:, :, 0:1], in_=PPv[:, :, TS:TS + 1]),
                            wait=((c_sc, evk[j]),))
                    pl.drain("vector")
                    pl.op("vector", lambda e, kc=kc: e.tensor_tensor_scan(
                        out=PPt[:, 0:FDP], data0=wpad[:, 0:FDP],
                        data1=kc[:, 0:FDP], initial=kc[:, 0:1],
                        op0=ALU.add, op1=ALU.max))
                    pl.drain("vector")
                    pl.op("vector", lambda e: e.tensor_tensor(
                        out=e1pv[:, :, 1:PD], in0=PPv[:, :, 0:TS],
                        in1=PPv[:, :, 1:PD], op=ALU.subtract), inc=(c_ve, 1))
                    d_v = c_ve.n
                    for hi in range(HI):
                        pl.op("scalar", lambda e, hi=hi: e.activation(
                            out=e1p[:, hi * PD + 1:(hi + 1) * PD],
                            in_=e1p[:, hi * PD + 1:(hi + 1) * PD], func=AF.Exp,
                            bias=wvec[:, hi:hi + 1]),
                            wait=((c_ve, d_v),) if hi == 0 else (),
                            inc=(c_sc, 1) if hi == HI - 1 else None)
                    e1b_s = c_sc.n
                    pl.op("vector", lambda e, kcv=kcv: e.tensor_tensor(
                        out=e2pv[:, :, 1:PD], in0=kcv[:, :, 1:PD],
                        in1=PPv[:, :, 1:PD], op=ALU.subtract), inc=(c_ve, 1))
                    e2ba_v = c_ve.n
                    pl.op("scalar", lambda e: e.activation(
                        out=e2p[:], in_=e2p[:], func=AF.Exp),
                        wait=((c_ve, e2ba_v),), inc=(c_sc, 1))
                    e2b_s = c_sc.n

                if 1 <= r < NT:
                    vcor = front_b()
                if 1 <= r <= NT:
                    wyw = [(c_gp, sry_gL[r - 2])] if r >= 2 else []
                    pl.op("vector", lambda e, kcv=kcv: e.tensor_tensor(
                        out=Bwv[:, :, :], in0=kcv[:, :, 1:PD],
                        in1=bcast_cols(vv["tf"]), op=ALU.add), wait=wyw)
                    pl.drain("vector")
                    pl.op("vector", lambda e: e.tensor_tensor(
                        out=Bqv[:, :, :], in0=Bwv[:, :, :],
                        in1=PPv[:, :, 0:TS], op=ALU.max))
                    pl.drain("vector")
                    pl.op("vector", lambda e: e.tensor_tensor(
                        out=tmpAv[:, :, :], in0=PPv[:, :, 0:TS],
                        in1=Bqv[:, :, :], op=ALU.subtract), inc=(c_ve, 1))
                    e1a_v = c_ve.n
                    pl.op("scalar", lambda e: e.activation(
                        out=tmpA[:], in_=tmpA[:], func=AF.Exp),
                        wait=((c_ve, e1a_v),), inc=(c_sc, 1))
                    e1_s = c_sc.n
                    pl.op("vector", lambda e: e.tensor_tensor(
                        out=Bwv[:, :, :], in0=Bwv[:, :, :], in1=Bqv[:, :, :],
                        op=ALU.subtract), inc=(c_ve, 1))
                    e2a_v = c_ve.n
                    pl.op("scalar", lambda e: e.activation(
                        out=Byw[:], in_=Byw[:], func=AF.Exp),
                        wait=((c_ve, e2a_v),), inc=(c_sc, 1))
                    e2_s = c_sc.n

                if r < NT:
                    w = [(c_pe, pjr[r]), (c_ve, vcor)]
                    if r >= 2:
                        w.append((c_gp, sry_gL[r - 2]))
                    for oj in range(HI):
                        pl.op("scalar", lambda e, oj=oj, r=r: e.activation(
                            out=srb[r % 2][:, oj * TS:(oj + 1) * TS],
                            in_=PS[:, PS_R + oj * TS:PS_R + (oj + 1) * TS],
                            func=AF.Sigmoid, bias=bias_r[:, oj:oj + 1]),
                            wait=w if oj == 0 else (),
                            inc=(c_sc, 1) if oj == HI - 1 else None)
                    evs[r] = c_sc.n
                    w = [(c_ve, mx_d[r]), (c_sc, evs[r])]
                    for oj in range(HI):
                        for hi in range(HI):
                            pl.op("tensor", lambda e, oj=oj, hi=hi: e.matmul(
                                PS[:, PS_R + oj * TS:PS_R + (oj + 1) * TS],
                                wv_res[:, hi * C + oj * 128:hi * C + oj * 128 + 128],
                                xvm[:, hi * TS:(hi + 1) * TS],
                                start=(hi == 0), stop=(hi == HI - 1)),
                                wait=w if (oj == 0 and hi == 0) else (),
                                inc=(c_pe, 1) if (oj == HI - 1 and hi == HI - 1)
                                else None)
                    pjv[r] = c_pe.n
                    for oj in range(4):
                        pl.op("scalar", lambda e, oj=oj, r=r: e.activation(
                            out=kp[r % 2][:, oj * PD + 1:(oj + 1) * PD],
                            in_=PS[:, k_off(oj):k_off(oj) + TS],
                            func=AF.Identity, bias=bias_k[:, oj:oj + 1]),
                            wait=((c_pe, pjk[r]),) if oj == 0 else (),
                            inc=(c_sc, 1) if oj == 3 else None)
                    evk1[r] = c_sc.n

                if 1 <= r <= NT:
                    j = r - 1
                    if jstart:
                        pl.op("vector", lambda e: e.memset(Ppv[:, :, 0:1], 0.0),
                              wait=((c_sc, e2b_s),))
                        pl.op("vector", lambda e: e.memset(e2pv[:, :, 0:1], 0.0))
                    else:
                        pl.op("vector", lambda e: e.tensor_copy(
                            out=Ppv[:, :, 0:1], in_=AAv[:, :, TS:TS + 1]),
                            wait=((c_sc, e2b_s),))
                        pl.op("vector", lambda e: e.tensor_copy(
                            out=e2pv[:, :, 0:1], in_=BBv[:, :, TS:TS + 1]))
                    pl.op("vector", lambda e, vcw=vcw: e.tensor_tensor(
                        out=Ppv[:, :, 1:PD], in0=e2pv[:, :, 1:PD],
                        in1=vcw[:, :, :], op=ALU.mult),
                        wait=((c_sc, e1b_s), (c_sc, evv[j])))
                    pl.drain("vector")
                    pl.op("vector", lambda e: e.tensor_tensor_scan(
                        out=BBt[:, 0:FDP], data0=e1p[:, 0:FDP],
                        data1=e2p[:, 0:FDP], initial=e2p[:, 0:1],
                        op0=ALU.mult, op1=ALU.add))
                    pl.drain("vector")
                    wd1 = [(c_sc, e1_s)]
                    if j >= 1:
                        wd1.append((c_sp_sry, sp_d[j - 1]))
                    pl.op("vector", lambda e: e.tensor_tensor(
                        out=sryb.rearrange("p (h t) -> p h t", h=HI)[:, :, :],
                        in0=tmpAv[:, :, :], in1=BBv[:, :, 0:TS],
                        op=ALU.mult), wait=wd1, inc=(c_ve, 1))
                    d1_v = c_ve.n
                    d1_vL[j] = d1_v
                    pl.op("vector", lambda e: e.tensor_tensor_scan(
                        out=AAt[:, 0:FDP], data0=e1p[:, 0:FDP],
                        data1=Pp[:, 0:FDP], initial=Pp[:, 0:1],
                        op0=ALU.mult, op1=ALU.add))
                    pl.drain("vector")
                    wn1 = [(c_sc, e1_s)]
                    if r < NT:
                        wn1.append((c_pe, pjv[r]))
                    pl.op("vector", lambda e: e.tensor_tensor(
                        out=xvm.rearrange("p (h t) -> p h t", h=HI)[:, :, :],
                        in0=tmpAv[:, :, :], in1=AAv[:, :, 0:TS],
                        op=ALU.mult), wait=wn1, inc=(c_ve, 1))
                    n1_v = c_ve.n
                    pl.op("gpsimd", lambda e, j=j: e.tensor_tensor(
                        out=Buq[:], in0=Byw[:], in1=vb[j % 2][:], op=ALU.mult),
                        wait=((c_sc, e2_s), (c_ve, e2a_v), (c_sc, evv[j])))
                    pl.drain("gpsimd")
                    pl.op("gpsimd", lambda e: e.tensor_tensor(
                        out=Byw[:], in0=sryb[:], in1=Byw[:], op=ALU.add),
                        wait=((c_ve, d1_v),), inc=(c_gp, 1))
                    den_g = c_gp.n
                    pl.drain("gpsimd")
                    pl.op("gpsimd", lambda e: e.tensor_tensor(
                        out=Buq[:], in0=xvm[:], in1=Buq[:], op=ALU.add),
                        wait=((c_ve, n1_v),), inc=(c_gp, 1))
                    num_gL[j] = c_gp.n
                    pl.drain("gpsimd")

                if r < NT:
                    wve = [(c_pe, pjv[r])]
                    if r >= 2:
                        wve.append((c_gp, num_gL[r - 2]))
                    for oj in range(HI):
                        pl.op("scalar", lambda e, oj=oj, r=r: e.activation(
                            out=vb[r % 2][:, oj * TS:(oj + 1) * TS],
                            in_=PS[:, PS_R + oj * TS:PS_R + (oj + 1) * TS],
                            func=AF.Identity, bias=bias_v[:, oj:oj + 1]),
                            wait=wve if oj == 0 else (),
                            inc=(c_sc, 1) if oj == HI - 1 else None)
                    evv[r] = c_sc.n
                    for oj in range(4, HI):
                        for hi in range(HI):
                            pl.op("tensor", lambda e, oj=oj, hi=hi: e.matmul(
                                PS[:, k_off(oj):k_off(oj) + TS],
                                wk_res[:, hi * C + oj * 128:hi * C + oj * 128 + 128],
                                xk[:, hi * TS:(hi + 1) * TS],
                                start=(hi == 0), stop=(hi == HI - 1)),
                                wait=((c_sc, evv[r]),) if (oj == 4 and hi == 0)
                                else (),
                                inc=(c_pe, 1) if (oj == HI - 1 and hi == HI - 1)
                                else None)
                    pjk2 = c_pe.n
                    for oj in range(4, HI):
                        pl.op("scalar", lambda e, oj=oj, r=r: e.activation(
                            out=kp[r % 2][:, oj * PD + 1:(oj + 1) * PD],
                            in_=PS[:, k_off(oj):k_off(oj) + TS],
                            func=AF.Identity, bias=bias_k[:, oj:oj + 1]),
                            wait=((c_pe, pjk2),) if oj == 4 else (),
                            inc=(c_sc, 1) if oj == HI - 1 else None)
                    evk[r] = c_sc.n
                    if r + 2 < NT:
                        load_x(r + 2)
                if 1 <= r <= NT:
                    j = r - 1
                    pl.op("scalar", lambda e: e.activation(
                        out=tmpA[:], in_=Byw[:], func=AF.Ln),
                        wait=((c_gp, den_g), (c_ve, n1_v)))
                    pl.drain("scalar")
                    pl.op("scalar", lambda e: e.activation(
                        out=tmpA[:], in_=tmpA[:], func=AF.Exp, scale=-1.0),
                        inc=(c_sc, 1))
                    rden_sL[j] = c_sc.n
        _phaseA()
        barrier()

        def _phaseB1():
          with contextlib.ExitStack() as ph:
            def sb(name, shape, dt=F32):
                return ph.enter_context(nc.sbuf_tensor("B_" + name, shape, dt))
            wo_res = sb("wo", [P, HI * C], BF16)
            fwk_res = sb("fwk", [P, HI * 4096], BF16)
            fwr_res = sb("fwr", [P, HI * C], BF16)
            with contextlib.ExitStack() as stg:
                def salloc(n, s, d=F32):
                    return stg.enter_context(nc.sbuf_tensor("Bs_" + n, s, d))
                stage_weights(salloc, [
                    (wd_in["woT"], C, wo_res, None, HI, None, 0, None, None),
                    (wd_in["fwkT"], 4096, fwk_res, vv["ln2_g"], HI, BFk, FHI,
                     bias_fk, cfk),
                    (wd_in["fwrT"], C, fwr_res, vv["ln2_g"], HI, BFr, HI,
                     bias_fr, cfr)])
            wB_ready = c_ve.n
            wcB = c_wc.n

            xB = [sb("x0", [P, FD]), sb("x1", [P, FD])]
            sryB = [sb("sry0", [P, FD], BF16), sb("sry1", [P, FD], BF16)]
            x2B = [sb("x2a", [P, FD]), sb("x2b", [P, FD])]
            xbf = sb("xbf", [P, FD], BF16)
            x2bf = sb("x2bf", [P, FD], BF16)
            mrow = sb("mrow", [P, TS]); varrow = sb("varrow", [P, TS])
            rstd = sb("rstd", [P, TS])
            tmpA = sb("tmpA", [P, FD])
            sbuf_s = tmpA
            zn2 = sb("zn2", [P, HI * (TS + 1)], BF16)
            dmix = sb("dmix", [P, FD], BF16)
            xk2 = [sb("xk2a", [P, FD], BF16), sb("xk2b", [P, FD], BF16)]
            xr2 = [sb("xr2a", [P, FD], BF16), sb("xr2b", [P, FD], BF16)]
            kkb = sb("kk", [P, KFD], BF16)
            r2b = sb("r2", [P, FD], BF16)
            bfk_c = sb("bfkc", [P, FHI])
            bfr_c = sb("bfrc", [P, HI])
            zn2v = zn2.rearrange("p (h t) -> p h t", h=HI)
            x2Bv = [b.rearrange("p (h t) -> p h t", h=HI) for b in x2B]
            tmpAv = tmpA.rearrange("p (h t) -> p h t", h=HI)
            dmv = dmix.rearrange("p (h t) -> p h t", h=HI)

            pl.drain("vector")
            pl.op("vector", lambda e: e.tensor_tensor(
                out=bfk_c[:], in0=bias_fk[:], in1=cfk[:], op=ALU.add),
                wait=((c_ve, wB_ready), (c_wc, wcB)))
            pl.op("vector", lambda e: e.tensor_tensor(
                out=bfr_c[:], in0=bias_fr[:], in1=cfr[:], op=ALU.add),
                inc=(c_ve, 1))
            bc_ready = c_ve.n

            PS_WO = 0
            PS_FK0 = 2048
            PS_ST0 = 3072

            x_ld = [0] * (NT + 3)
            x2e = [0] * (NT + 2); x2f = [0] * (NT + 2); sq2 = [0] * (NT + 2)
            st2 = [0] * (NT + 1); vv2 = [0] * (NT + 1)
            mxk = [0] * (NT + 1); mxr = [0] * (NT + 1)
            wo_pe = [0] * (NT + 2)
            fk_ev = [0] * (NT + 1)
            r2_ev = [0] * (NT + 1)
            vsq_prev = [0, 0, 0, 0]

            def load_b1(r, xw, sw):
                slx, sls = c_x[r % 2], c_sry[r % 2]
                pl.op("sync", lambda e, r=r: e.dma_start(out=xB[r % 2][:],
                                                         in_=x_in[r, :, :]),
                      wait=xw, inc=(slx, 16))
                pl.op("sync", lambda e, r=r: e.dma_start(out=sryB[r % 2][:],
                                                         in_=sry_s[r, :, :]),
                      wait=sw, inc=(sls, 16))
                x_ld[r] = ((slx, slx.n), (sls, sls.n))

            def emit_wo(j):
                w = list(x_ld[j])
                if j >= 1:
                    w.append((c_ve, x2e[j - 1]))
                if j == 0:
                    w += [(c_ve, wB_ready), (c_wc, wcB)]
                for oj in range(HI):
                    for hi in range(HI):
                        pl.op("tensor", lambda e, oj=oj, hi=hi, j=j: e.matmul(
                            PS[:, PS_WO + oj * TS:PS_WO + (oj + 1) * TS],
                            wo_res[:, hi * C + oj * 128:hi * C + oj * 128 + 128],
                            sryB[j % 2][:, hi * TS:(hi + 1) * TS],
                            start=(hi == 0), stop=(hi == HI - 1)),
                            wait=w if (oj == 0 and hi == 0) else (),
                            inc=(c_pe, 1) if (oj == HI - 1 and hi == HI - 1)
                            else None)
                wo_pe[j] = c_pe.n

            def emit_stats(j):
                base = PS_ST0 + (j % 2) * 512
                wst = [(c_ve, max(sq2[j], mxr[j - 1] if j >= 1 else 0))]
                for hi in range(HI):
                    pl.op("tensor", lambda e, hi=hi, base=base: e.matmul(
                        PS[:, base:base + TS], ones[:, :],
                        xbf[:, hi * TS:(hi + 1) * TS], start=(hi == 0),
                        stop=(hi == HI - 1)), wait=wst if hi == 0 else ())
                for hi in range(HI):
                    pl.op("tensor", lambda e, hi=hi, base=base: e.matmul(
                        PS[:, base + TS:base + 2 * TS], ones[:, :],
                        x2bf[:, hi * TS:(hi + 1) * TS], start=(hi == 0),
                        stop=(hi == HI - 1)),
                        inc=(c_pe, 1) if hi == HI - 1 else None)
                st2[j] = c_pe.n

            def emit_x2blk(j):
                pl.drain("vector")
                for h in range(2):
                    pl.op("vector", lambda e, j=j, h=h: e.tensor_tensor(
                        out=x2B[j % 2][:, h * 1024:(h + 1) * 1024],
                        in0=PS[:, PS_WO + h * 1024:PS_WO + (h + 1) * 1024],
                        in1=xB[j % 2][:, h * 1024:(h + 1) * 1024], op=ALU.add),
                        wait=((c_pe, wo_pe[j]),) if h == 0 else (),
                        inc=(c_ve, 1) if h == 1 else None)
                x2e[j] = c_ve.n
                pl.drain("vector")
                w = [(c_sp_s, 16 * j)] if j >= 1 else []
                pl.op("vector", lambda e, j=j: e.tensor_tensor(
                    out=sbuf_s[:], in0=x2B[j % 2][:], in1=xB[j % 2][:],
                    op=ALU.add), wait=w, inc=(c_ve, 1))
                x2f[j] = c_ve.n
                pl.op("gpsimd", lambda e, j=j: e.dma_start(out=s_s[j, :, :],
                                                           in_=sbuf_s[:]),
                      wait=((c_ve, x2f[j]),), inc=(c_sp_s, 16))
                w = [(c_pe, st2[j - 1])] if j >= 1 else []
                pl.op("vector", lambda e, j=j: e.tensor_copy(
                    out=xbf[:], in_=x2B[j % 2][:]), wait=w)
                pl.drain("vector")
                pl.op("vector", lambda e: e.tensor_tensor(
                    out=x2bf[:], in0=xbf[:], in1=xbf[:], op=ALU.mult),
                    inc=(c_ve, 1))
                sq2[j] = c_ve.n

            def emit_front_pre(j):
                base = PS_ST0 + (j % 2) * 512
                pl.op("vector", lambda e, base=base: e.tensor_scalar(
                    out=mrow[:], in0=PS[:, base:base + TS], scalar1=1.0 / C,
                    scalar2=None, op0=ALU.mult), wait=((c_pe, st2[j]),))
                pl.drain("vector")
                pl.op("vector", lambda e: e.tensor_tensor(
                    out=varrow[:], in0=mrow[:], in1=mrow[:], op=ALU.mult))
                pl.drain("vector")
                pl.op("vector", lambda e, base=base: e.scalar_tensor_tensor(
                    out=varrow[:], in0=PS[:, base + TS:base + 2 * TS],
                    scalar=1.0 / C, in1=varrow[:], op0=ALU.mult,
                    op1=ALU.subtract), inc=(c_ve, 1))
                vv2[j] = c_ve.n

            def emit_rstd(j):
                pl.op("scalar", lambda e: e.activation(
                    out=rstd[:], in_=varrow[:], func=AF.Ln, bias=epsb[:, 0:1]),
                    wait=((c_ve, vv2[j]),))
                pl.drain("scalar")
                pl.op("scalar", lambda e: e.activation(
                    out=rstd[:], in_=rstd[:], func=AF.Exp, scale=-0.5),
                    inc=(c_sc, 1))
                return c_sc.n

            def emit_front_post(j, rs2):
                sstart = (j % NTT == 0)
                if sstart:
                    pl.op("vector", lambda e: e.memset(zn2v[:, :, 0:1], 0.0))
                else:
                    pl.drain("vector")
                    pl.op("vector", lambda e: e.tensor_copy(
                        out=zn2v[:, :, 0:1], in_=zn2v[:, :, TS:TS + 1]))
                pl.op("vector", lambda e, j=j: e.tensor_tensor(
                    out=tmpAv[:, :, :], in0=x2Bv[j % 2][:, :, :],
                    in1=bcast_rows(mrow), op=ALU.subtract),
                    wait=((c_sp_s, 16 * (j + 1)),))
                pl.drain("vector")
                pl.op("vector", lambda e: e.tensor_tensor(
                    out=zn2v[:, :, 1:TS + 1], in0=tmpAv[:, :, :],
                    in1=bcast_rows(rstd), op=ALU.mult),
                    wait=((c_sc, rs2),))
                pl.drain("vector")
                pl.op("vector", lambda e: e.tensor_tensor(
                    out=dmv[:, :, :], in0=zn2v[:, :, 1:TS + 1],
                    in1=zn2v[:, :, 0:TS], op=ALU.subtract))
                pl.drain("vector")
                par = j % 2
                for hi in range(HI):
                    pl.op("vector", lambda e, hi=hi, par=par:
                          e.scalar_tensor_tensor(
                        out=xk2[par][:, hi * TS:(hi + 1) * TS],
                        in0=dmix[:, hi * TS:(hi + 1) * TS],
                        scalar=vv["ftmk"][:, hi:hi + 1],
                        in1=zn2v[:, hi, 0:TS], op0=ALU.mult, op1=ALU.add),
                        inc=(c_ve, 1) if hi == HI - 1 else None)
                mxk[j] = c_ve.n
                for hi in range(HI):
                    pl.op("vector", lambda e, hi=hi, par=par:
                          e.scalar_tensor_tensor(
                        out=xr2[par][:, hi * TS:(hi + 1) * TS],
                        in0=dmix[:, hi * TS:(hi + 1) * TS],
                        scalar=vv["ftmr"][:, hi:hi + 1],
                        in1=zn2v[:, hi, 0:TS], op0=ALU.mult, op1=ALU.add),
                        inc=(c_ve, 1) if hi == HI - 1 else None)
                mxr[j] = c_ve.n

            gpe = [0] * 20
            evd = [0] * 20

            def mm_group(g, r):
                par = g % 2
                base = PS_FK0 + par * 512
                is_r = g >= 16
                w = []
                if g == 0:
                    w.append((c_ve, mxk[r]))
                    if r >= 1:
                        w.append((c_sc, r2_ev[r - 1]))
                elif g == 16:
                    w += [(c_ve, mxr[r]), (c_sc, evd[14])]
                elif g == 17:
                    w.append((c_sc, evd[15]))
                elif g >= 2:
                    w.append((c_sc, evd[g - 2]))
                src = xr2[r % 2] if is_r else xk2[r % 2]
                wres = fwr_res if is_r else fwk_res
                wstep = C if is_r else 4096
                for ojl in range(2):
                    oj = (g - 16) * 2 + ojl if is_r else g * 2 + ojl
                    for hi in range(HI):
                        pl.op("tensor", lambda e, oj=oj, ojl=ojl, hi=hi,
                              base=base, src=src, wres=wres, wstep=wstep:
                              e.matmul(
                            PS[:, base + ojl * TS:base + (ojl + 1) * TS],
                            wres[:, hi * wstep + oj * 128:
                                 hi * wstep + oj * 128 + 128],
                            src[:, hi * TS:(hi + 1) * TS],
                            start=(hi == 0), stop=(hi == HI - 1)),
                            wait=w if (ojl == 0 and hi == 0) else (),
                            inc=(c_pe, 1) if (ojl == 1 and hi == HI - 1)
                            else None)
                gpe[g] = c_pe.n
                sstart = (r % NTT == 0)
                outb = r2b if is_r else kkb
                bias = bias_fr if is_r else bias_fk
                biasc = bfr_c if is_r else bfk_c
                fn = AF.Sigmoid if is_r else AF.Relu
                first = True
                for ojl in range(2):
                    oj = (g - 16) * 2 + ojl if is_r else g * 2 + ojl
                    pcol = base + ojl * TS
                    wz = []
                    if first:
                        wz.append((c_pe, gpe[g]))
                        if not is_r and g % 4 == 0 and r >= 1:
                            wz.append((c_ve, vsq_prev[g // 4]))
                        if g == 0 and r >= 1:
                            wz.append((c_sp_kk, 16 * r))
                        if g == 16 and r >= 1:
                            wz.append((c_sp_r2, 16 * r))
                    last = (ojl == 1)
                    if sstart:
                        pl.op("scalar", lambda e, oj=oj, pcol=pcol, outb=outb,
                              biasc=biasc, fn=fn: e.activation(
                            out=outb[:, oj * TS:oj * TS + 1],
                            in_=PS[:, pcol:pcol + 1], func=fn,
                            bias=biasc[:, oj:oj + 1]),
                            wait=wz + [(c_ve, bc_ready)])
                        wz = []
                        pl.op("scalar", lambda e, oj=oj, pcol=pcol, outb=outb,
                              bias=bias, fn=fn: e.activation(
                            out=outb[:, oj * TS + 1:(oj + 1) * TS],
                            in_=PS[:, pcol + 1:pcol + TS], func=fn,
                            bias=bias[:, oj:oj + 1]),
                            inc=(c_sc, 1) if last else None)
                    else:
                        pl.op("scalar", lambda e, oj=oj, pcol=pcol, outb=outb,
                              bias=bias, fn=fn: e.activation(
                            out=outb[:, oj * TS:(oj + 1) * TS],
                            in_=PS[:, pcol:pcol + TS], func=fn,
                            bias=bias[:, oj:oj + 1]),
                            wait=wz,
                            inc=(c_sc, 1) if last else None)
                    first = False
                evd[g] = c_sc.n

            load_b1(0, [(c_ve, wB_ready), (c_wc, wcB)], [(c_sp_sry, 16 * NT)])
            load_b1(1, [(c_ve, wB_ready)], [])
            emit_wo(0)
            emit_x2blk(0)
            emit_stats(0)
            emit_front_pre(0)
            emit_front_post(0, emit_rstd(0))
            emit_wo(1)
            emit_x2blk(1)
            if NT > 2:
                load_b1(2, [(c_ve, x2f[0])], [(c_pe, wo_pe[0])])

            for r in range(NT):
                if r + 2 < NT:
                    emit_wo(r + 2)
                if r + 1 < NT:
                    emit_stats(r + 1)
                for g in range(6):
                    mm_group(g, r)
                if r + 1 < NT:
                    emit_front_pre(r + 1)
                    emit_front_post(r + 1, emit_rstd(r + 1))
                for g in range(6, 20):
                    mm_group(g, r)
                r2_ev[r] = evd[19]
                pl.op("gpsimd", lambda e, r=r: e.dma_start(out=r2_s[r, :, :],
                                                           in_=r2b[:]),
                      wait=((c_sc, r2_ev[r]),), inc=(c_sp_r2, 16))
                if r + 2 < NT:
                    emit_x2blk(r + 2)
                vsq_cur = [0, 0, 0, 0]
                for m in range(4):
                    pl.op("vector", lambda e, m=m: e.tensor_tensor(
                        out=kkb[:, m * 2048:(m + 1) * 2048],
                        in0=kkb[:, m * 2048:(m + 1) * 2048],
                        in1=kkb[:, m * 2048:(m + 1) * 2048], op=ALU.mult),
                        wait=((c_sc, evd[4 * m + 3]),), inc=(c_ve, 1))
                    vsq_cur[m] = c_ve.n
                vsq_prev = vsq_cur
                fk_ev[r] = vsq_cur[3]
                pl.op("gpsimd", lambda e, r=r: e.dma_start(out=kk_s[r, :, :],
                                                           in_=kkb[:]),
                      wait=((c_ve, fk_ev[r]),), inc=(c_sp_kk, 16))
                if r + 3 < NT:
                    load_b1(r + 3, [(c_ve, x2f[r + 1])], [(c_pe, wo_pe[r + 1])])
        _phaseB1()
        barrier()

        def _phaseB2():
          with contextlib.ExitStack() as ph:
            def sb(name, shape, dt=F32):
                return ph.enter_context(nc.sbuf_tensor("C_" + name, shape, dt))
            fwv_res = sb("fwv", [P, FHI * C], BF16)
            with contextlib.ExitStack() as stg:
                def salloc(n, s, d=F32):
                    return stg.enter_context(nc.sbuf_tensor("Cs_" + n, s, d))
                stage_weights(salloc, [
                    (wd_in["fwvT"], C, fwv_res, None, FHI, None, 0, None, None)])
            wcC = c_wc.n

            kkB = [sb("kk0", [P, KFD], BF16), sb("kk1", [P, KFD], BF16)]
            r2B = [sb("r20", [P, FD], BF16), sb("r21", [P, FD], BF16)]
            sB = [sb("s0", [P, FD]), sb("s1", [P, FD])]
            out1B = [sb("o10", [P, FD]), sb("o11", [P, FD])]
            xbfB = [sb("xbf0", [P, FD], BF16), sb("xbf1", [P, FD], BF16)]
            x2bfB = [sb("x2bf0", [P, FD], BF16), sb("x2bf1", [P, FD], BF16)]
            mrow = sb("mrow", [P, TS]); varrow = sb("varrow", [P, TS])
            rstd = sb("rstd", [P, TS])
            tmpA = sb("tmpA", [P, FD])
            x3buf = sb("x3", [P, FD])
            o1Bv = [b.rearrange("p (h t) -> p h t", h=HI) for b in out1B]
            tmpAv = tmpA.rearrange("p (h t) -> p h t", h=HI)
            x3v = x3buf.rearrange("p (h t) -> p h t", h=HI)

            PS_RKV0 = 0
            PS_ST2 = 2048
            ld2 = [0] * (NT + 2)
            use2 = [0] * (NT + 1)
            sq3 = [0] * (NT + 1); st3 = [0] * (NT + 1)
            vv3 = [0] * (NT + 1)
            sp3 = [0] * (NT + 1)
            evd2 = [0, 0, 0, 0]
            rkv_g = [0, 0, 0, 0]

            def load_b2(r, first=False):
                w = ((c_wc, wcC),) if first else ((c_ve, use2[r - 2]),)
                sl = c_b2[r % 2]
                pl.op("sync", lambda e, r=r: e.dma_start(out=kkB[r % 2][:],
                                                         in_=kk_s[r, :, :]),
                      wait=w, inc=(sl, 16))
                pl.op("sync", lambda e, r=r: e.dma_start(out=r2B[r % 2][:],
                                                         in_=r2_s[r, :, :]),
                      inc=(sl, 16))
                pl.op("sync", lambda e, r=r: e.dma_start(out=sB[r % 2][:],
                                                         in_=s_s[r, :, :]),
                      inc=(sl, 16))
                ld2[r] = (sl, sl.n)

            def emit_stats3(j):
                base = PS_ST2 + (j % 2) * 512
                wst = [(c_ve, max(sq3[j], vv3[j - 2] if j >= 2 else 0))]
                for hi in range(HI):
                    pl.op("tensor", lambda e, hi=hi, base=base, j=j: e.matmul(
                        PS[:, base:base + TS], ones[:, :],
                        xbfB[j % 2][:, hi * TS:(hi + 1) * TS], start=(hi == 0),
                        stop=(hi == HI - 1)), wait=wst if hi == 0 else ())
                for hi in range(HI):
                    pl.op("tensor", lambda e, hi=hi, base=base, j=j: e.matmul(
                        PS[:, base + TS:base + 2 * TS], ones[:, :],
                        x2bfB[j % 2][:, hi * TS:(hi + 1) * TS], start=(hi == 0),
                        stop=(hi == HI - 1)),
                        inc=(c_pe, 1) if hi == HI - 1 else None)
                st3[j] = c_pe.n

            def emit_front3(j):
                base = PS_ST2 + (j % 2) * 512
                pl.drain("vector")
                pl.op("vector", lambda e, base=base: e.tensor_scalar(
                    out=mrow[:], in0=PS[:, base:base + TS], scalar1=1.0 / C,
                    scalar2=None, op0=ALU.mult), wait=((c_pe, st3[j]),))
                pl.drain("vector")
                pl.op("vector", lambda e: e.tensor_tensor(
                    out=varrow[:], in0=mrow[:], in1=mrow[:], op=ALU.mult))
                pl.drain("vector")
                pl.op("vector", lambda e, base=base: e.scalar_tensor_tensor(
                    out=varrow[:], in0=PS[:, base + TS:base + 2 * TS],
                    scalar=1.0 / C, in1=varrow[:], op0=ALU.mult,
                    op1=ALU.subtract), inc=(c_ve, 1))
                vv3[j] = c_ve.n
                pl.op("scalar", lambda e: e.activation(
                    out=rstd[:], in_=varrow[:], func=AF.Ln, bias=epsb[:, 0:1]),
                    wait=((c_ve, vv3[j]),))
                pl.drain("scalar")
                pl.op("scalar", lambda e: e.activation(
                    out=rstd[:], in_=rstd[:], func=AF.Exp, scale=-0.5),
                    inc=(c_sc, 1))
                rsv = c_sc.n
                w = [(c_sc, rsv)]
                if j >= 1:
                    w.append((c_sp_x3, sp3[j - 1]))
                pl.op("vector", lambda e, j=j: e.tensor_tensor(
                    out=tmpAv[:, :, :], in0=o1Bv[j % 2][:, :, :],
                    in1=bcast_rows(mrow), op=ALU.subtract), wait=w)
                pl.drain("vector")
                pl.op("vector", lambda e: e.tensor_tensor(
                    out=x3v[:, :, :], in0=tmpAv[:, :, :], in1=bcast_rows(rstd),
                    op=ALU.mult), inc=(c_ve, 1))
                pl.op("gpsimd", lambda e, j=j: e.dma_start(out=x3_s[j, :, :],
                                                           in_=x3buf[:]),
                      wait=((c_ve, c_ve.n),), inc=(c_sp_x3, 16))
                sp3[j] = c_sp_x3.n

            load_b2(0, True)
            if NT > 1:
                load_b2(1, True)

            for r in range(NT):
                pl.drain("vector")
                for g in range(4):
                    base = PS_RKV0 + (g % 2) * 512
                    w = []
                    if g == 0:
                        w.append(ld2[r])
                        if r >= 1:
                            w.append((c_ve, evd2[2]))
                    elif g == 1:
                        if r >= 1:
                            w.append((c_ve, evd2[3]))
                    else:
                        w.append((c_ve, evd2[g - 2]))
                    for ojl in range(2):
                        oj = g * 2 + ojl
                        for khi in range(FHI):
                            pl.op("tensor", lambda e, oj=oj, ojl=ojl, khi=khi,
                                  r=r, base=base: e.matmul(
                                PS[:, base + ojl * TS:base + (ojl + 1) * TS],
                                fwv_res[:, khi * C + oj * 128:
                                        khi * C + oj * 128 + 128],
                                kkB[r % 2][:, khi * TS:(khi + 1) * TS],
                                start=(khi == 0), stop=(khi == FHI - 1)),
                                wait=w if (ojl == 0 and khi == 0) else (),
                                inc=(c_pe, 1) if (ojl == 1 and khi == FHI - 1)
                                else None)
                    rkv_g[g] = c_pe.n
                    pl.op("vector", lambda e, g=g, r=r, base=base:
                          e.tensor_tensor(
                        out=tmpA[:, g * 512:(g + 1) * 512],
                        in0=PS[:, base:base + 512],
                        in1=r2B[r % 2][:, g * 512:(g + 1) * 512],
                        op=ALU.mult),
                        wait=((c_pe, rkv_g[g]),), inc=(c_ve, 1))
                    evd2[g] = c_ve.n
                pl.drain("vector")
                pl.op("vector", lambda e, r=r: e.tensor_tensor(
                    out=out1B[r % 2][:], in0=tmpA[:], in1=sB[r % 2][:],
                    op=ALU.add), inc=(c_ve, 1))
                use2[r] = c_ve.n
                if r + 2 < NT:
                    load_b2(r + 2)
                pl.drain("vector")
                pl.op("vector", lambda e, r=r: e.tensor_copy(
                    out=xbfB[r % 2][:], in_=out1B[r % 2][:]),
                    wait=[(c_pe, st3[r - 2])] if r >= 2 else [])
                pl.drain("vector")
                pl.op("vector", lambda e, r=r: e.tensor_tensor(
                    out=x2bfB[r % 2][:], in0=xbfB[r % 2][:],
                    in1=xbfB[r % 2][:], op=ALU.mult), inc=(c_ve, 1))
                sq3[r] = c_ve.n
                if r >= 1:
                    emit_front3(r - 1)
                if r >= 1:
                    emit_stats3(r)
                else:
                    emit_stats3(0)
            emit_front3(NT - 1)
        _phaseB2()
        barrier()

        def _phaseC():
          with contextlib.ExitStack() as ph:
            def sb(name, shape, dt=F32):
                return ph.enter_context(nc.sbuf_tensor("D_" + name, shape, dt))
            wg_res = sb("wg", [P, HI * 3072], BF16)
            wu_res = sb("wu", [P, HI * 3072], BF16)
            wd_res = sb("wd", [P, GHI * C], BF16)
            with contextlib.ExitStack() as stg:
                def salloc(n, s, d=F32):
                    return stg.enter_context(nc.sbuf_tensor("Ds_" + n, s, d))
                stage_weights(salloc, [
                    (wd_in["wgT"], 3072, wg_res, None, HI, None, 0, None, None),
                    (wd_in["wuT"], 3072, wu_res, None, HI, None, 0, None, None),
                    (wd_in["wdT"], C, wd_res, None, GHI, None, 0, None, None)])
            wcD = c_wc.n

            x3C = [sb("x30", [P, FD]), sb("x31", [P, FD])]
            x3bf = sb("x3bf", [P, FD], BF16)
            hbuf = sb("h", [P, GFD], BF16)
            gtmp = sb("gt", [P, TS])
            lnin = sb("lnin", [P, FD])
            xbf = sb("xbf", [P, FD], BF16)
            x2bf = sb("x2bf", [P, FD], BF16)
            mrow = sb("mrow", [P, TS]); varrow = sb("varrow", [P, TS])
            rstd = sb("rstd", [P, TS])
            tmpA = sb("tmpA", [P, FD])
            lnv = lnin.rearrange("p (h t) -> p h t", h=HI)
            tmpAv = tmpA.rearrange("p (h t) -> p h t", h=HI)

            PS_G = 2048
            PS_U = 3072
            PS_D = 0
            PS_ST3 = 2048

            ld3 = [0] * (NT + 2)
            use3 = [0] * (NT + 1)
            sq4 = [0] * (NT + 1); st4 = [0] * (NT + 1)
            mn4 = [0] * (NT + 1); vv4 = [0] * (NT + 1)
            dn_d = [0] * (NT + 1)
            out_sp = [0] * (NT + 1)
            h_ev = {}

            def load_c(r, first=False):
                w = ((c_wc, wcD),) if first else ((c_ve, use3[r - 2]),)
                sl = c_x[r % 2]
                pl.op("sync", lambda e, r=r: e.dma_start(out=x3C[r % 2][:],
                                                         in_=x3_s[r, :, :]),
                      wait=w, inc=(sl, 16))
                ld3[r] = (sl, sl.n)
            load_c(0, True)
            if NT > 1:
                load_c(1, True)

            cast_s = [0] * (NT + 1)

            def emit_cast3(t, wextra=()):
                w = [ld3[t]] + list(wextra)
                pl.op("scalar", lambda e, t=t: e.activation(
                    out=x3bf[:], in_=x3C[t % 2][:], func=AF.Identity),
                    wait=w, inc=(c_sc, 1))
                cast_s[t] = c_sc.n

            emit_cast3(0)
            for r in range(NT + 1):
                if r < NT:
                    for oj in range(GHI):
                        par = oj % 2
                        wg0 = [(c_sc, cast_s[r])]
                        if oj >= 2:
                            wg0.append((c_ve, h_ev[(r, oj - 2)]))
                        elif r >= 1:
                            wg0.append((c_ve, h_ev[(r - 1, GHI - 2 + oj)]))
                            if oj == 0 and r >= 2:
                                wg0 += [(c_sc, mn4[r - 2]), (c_ve, vv4[r - 2])]
                        for hi in range(HI):
                            pl.op("tensor", lambda e, oj=oj, hi=hi, par=par: e.matmul(
                                PS[:, PS_G + par * 512:PS_G + par * 512 + TS],
                                wg_res[:, hi * 3072 + oj * 128:hi * 3072 + oj * 128 + 128],
                                x3bf[:, hi * TS:(hi + 1) * TS],
                                start=(hi == 0), stop=(hi == HI - 1)),
                                wait=wg0 if hi == 0 else ())
                        for hi in range(HI):
                            pl.op("tensor", lambda e, oj=oj, hi=hi, par=par: e.matmul(
                                PS[:, PS_U + par * 512:PS_U + par * 512 + TS],
                                wu_res[:, hi * 3072 + oj * 128:hi * 3072 + oj * 128 + 128],
                                x3bf[:, hi * TS:(hi + 1) * TS],
                                start=(hi == 0), stop=(hi == HI - 1)),
                                inc=(c_pe, 1) if hi == HI - 1 else None)
                        gu_pe = c_pe.n
                        wsil = [(c_pe, gu_pe)]
                        if oj >= 1:
                            wsil.append((c_ve, h_ev[(r, oj - 1)]))
                        pl.op("scalar", lambda e, par=par: e.activation(
                            out=gtmp[:], in_=PS[:, PS_G + par * 512:PS_G + par * 512 + TS],
                            func=AF.Silu, scale=float(INV127)),
                            wait=wsil, inc=(c_sc, 1))
                        sild = c_sc.n
                        pl.op("vector", lambda e, oj=oj, par=par: e.tensor_tensor(
                            out=hbuf[:, oj * TS:(oj + 1) * TS], in0=gtmp[:],
                            in1=PS[:, PS_U + par * 512:PS_U + par * 512 + TS],
                            op=ALU.mult),
                            wait=((c_sc, sild),), inc=(c_ve, 1))
                        h_ev[(r, oj)] = c_ve.n
                    if r + 1 < NT:
                        emit_cast3(r + 1, wextra=((c_pe, gu_pe),))
                    w = [(c_ve, h_ev[(r, GHI - 1)])]
                    if r >= 1:
                        w.append((c_ve, use3[r - 1]))
                    for oj in range(HI):
                        for khi in range(GHI):
                            pl.op("tensor", lambda e, oj=oj, khi=khi: e.matmul(
                                PS[:, PS_D + oj * TS:PS_D + (oj + 1) * TS],
                                wd_res[:, khi * C + oj * 128:khi * C + oj * 128 + 128],
                                hbuf[:, khi * TS:(khi + 1) * TS],
                                start=(khi == 0), stop=(khi == GHI - 1)),
                                wait=w if (oj == 0 and khi == 0) else (),
                                inc=(c_pe, 1) if (oj == HI - 1 and khi == GHI - 1)
                                else None)
                    dn_d[r] = c_pe.n

                if r >= 1:
                    j = r - 1
                    pl.drain("vector")
                    w = []
                    if j >= 1:
                        w += [(c_sc, sq4[j - 1]), (c_pe, st4[j - 1])]
                    pl.op("vector", lambda e: e.tensor_copy(out=xbf[:], in_=lnin[:]),
                          wait=w, inc=(c_ve, 1))
                    castv2 = c_ve.n
                    pl.op("scalar", lambda e: e.activation(out=x2bf[:], in_=xbf[:],
                                                           func=AF.Square),
                          wait=((c_ve, castv2),), inc=(c_sc, 1))
                    sq4[j] = c_sc.n
                    wst = [(c_sc, sq4[j])]
                    if j >= 1:
                        wst += [(c_ve, vv4[j - 1]), (c_sc, mn4[j - 1])]
                    for hi in range(HI):
                        pl.op("tensor", lambda e, hi=hi: e.matmul(
                            PS[:, PS_ST3:PS_ST3 + TS], ones[:, :],
                            xbf[:, hi * TS:(hi + 1) * TS], start=(hi == 0),
                            stop=(hi == HI - 1)), wait=wst if hi == 0 else ())
                    for hi in range(HI):
                        pl.op("tensor", lambda e, hi=hi: e.matmul(
                            PS[:, PS_ST3 + TS:PS_ST3 + 2 * TS], ones[:, :],
                            x2bf[:, hi * TS:(hi + 1) * TS], start=(hi == 0),
                            stop=(hi == HI - 1)),
                            inc=(c_pe, 1) if hi == HI - 1 else None)
                    st4[j] = c_pe.n
                    pl.op("scalar", lambda e: e.activation(
                        out=mrow[:], in_=PS[:, PS_ST3:PS_ST3 + TS], func=AF.Copy,
                        scale=1.0 / C), wait=((c_pe, st4[j]),), inc=(c_sc, 1))
                    mn4[j] = c_sc.n
                    pl.op("vector", lambda e: e.tensor_tensor(
                        out=varrow[:], in0=mrow[:], in1=mrow[:], op=ALU.mult),
                        wait=((c_sc, mn4[j]),))
                    pl.drain("vector")
                    pl.op("vector", lambda e: e.scalar_tensor_tensor(
                        out=varrow[:], in0=PS[:, PS_ST3 + TS:PS_ST3 + 2 * TS],
                        scalar=1.0 / C, in1=varrow[:], op0=ALU.mult,
                        op1=ALU.subtract), inc=(c_ve, 1))
                    vv4[j] = c_ve.n
                    pl.op("scalar", lambda e: e.activation(
                        out=rstd[:], in_=varrow[:], func=AF.Ln, bias=epsb[:, 0:1]),
                        wait=((c_ve, vv4[j]),))
                    pl.drain("scalar")
                    pl.op("scalar", lambda e: e.activation(
                        out=rstd[:], in_=rstd[:], func=AF.Exp, scale=-0.5),
                        inc=(c_sc, 1))
                    rsv = c_sc.n
                    w = [(c_sc, rsv)]
                    if j >= 1:
                        w.append((c_sp_out, out_sp[j - 1]))
                    pl.op("vector", lambda e: e.tensor_tensor(
                        out=tmpAv[:, :, :], in0=lnv[:, :, :], in1=bcast_rows(mrow),
                        op=ALU.subtract), wait=w)
                    pl.drain("vector")
                    pl.op("vector", lambda e: e.tensor_tensor(
                        out=tmpAv[:, :, :], in0=tmpAv[:, :, :], in1=bcast_rows(rstd),
                        op=ALU.mult), inc=(c_ve, 1))
                    pl.op("gpsimd", lambda e, j=j: e.dma_start(out=out_d[j, :, :],
                                                               in_=tmpA[:]),
                          wait=((c_ve, c_ve.n),), inc=(c_sp_out, 16))
                    out_sp[j] = c_sp_out.n

                if r < NT:
                    pl.drain("vector")
                    for oj in range(HI):
                        pl.op("vector", lambda e, oj=oj, r=r: e.scalar_tensor_tensor(
                            out=lnin[:, oj * TS:(oj + 1) * TS],
                            in0=PS[:, PS_D + oj * TS:PS_D + (oj + 1) * TS],
                            scalar=float(INV127 * INV127),
                            in1=x3C[r % 2][:, oj * TS:(oj + 1) * TS],
                            op0=ALU.mult, op1=ALU.add),
                            wait=((c_pe, dn_d[r]),) if oj == 0 else (),
                            inc=(c_ve, 1) if oj == HI - 1 else None)
                    use3[r] = c_ve.n
                    if r + 2 < NT:
                        load_c(r + 2)
            pl.wait("gpsimd", c_sp_out, c_sp_out.n)
        _phaseC()

        pl.emit(nc)
    return nc


def _shape_x(x_core, NTT, TS):
    nseq, T, Cc = x_core.shape
    r = x_core.reshape(nseq, NTT, TS, HI, P).transpose(0, 1, 4, 3, 2)
    return np.ascontiguousarray(r.reshape(nseq * NTT, P, HI * TS), dtype=np.float32)


def _unshape_out(o, NTT, TS, nseq):
    r = o.reshape(nseq, NTT, P, HI, TS).transpose(0, 1, 4, 3, 2)
    return r.reshape(nseq, NTT * TS, HI * P)


def kernel(**inputs):
    T, TS = 4096, 256
    NTT = T // TS
    x = np.ascontiguousarray(np.asarray(inputs["L0_attn_x"]), dtype=np.float32)
    B = x.shape[0]
    nseq = B // NCORES

    def tr(a, dt):
        return np.ascontiguousarray(np.asarray(a).T, dt)

    common = {
        "wkT": tr(inputs["L0_attn_Wk"], np.float32),
        "wvT": tr(inputs["L0_attn_Wv"], np.float32),
        "wrT": tr(inputs["L0_attn_Wr"], np.float32),
        "woT": tr(inputs["L0_attn_Wo"], np.float32),
        "fwkT": tr(inputs["L0_attn_fWk"], np.float32),
        "fwrT": tr(inputs["L0_attn_fWr"], np.float32),
        "fwvT": tr(inputs["L0_attn_fWv"], np.float32),
        "wgT": tr(inputs["L0_ffn_w_gate"], np.int32),
        "wuT": tr(inputs["L0_ffn_w_up"], np.int32),
        "wdT": tr(inputs["L0_ffn_w_down"], np.int32),
        "ln1_g": np.ascontiguousarray(np.asarray(inputs["L0_attn_ln1_g"]).reshape(-1), np.float32),
        "ln1_b": np.ascontiguousarray(np.asarray(inputs["L0_attn_ln1_b"]).reshape(-1), np.float32),
        "ln2_g": np.ascontiguousarray(np.asarray(inputs["L0_attn_ln2_g"]).reshape(-1), np.float32),
        "ln2_b": np.ascontiguousarray(np.asarray(inputs["L0_attn_ln2_b"]).reshape(-1), np.float32),
        "td": np.ascontiguousarray(np.asarray(inputs["L0_attn_time_decay"]).reshape(-1), np.float32),
        "tf": np.ascontiguousarray(np.asarray(inputs["L0_attn_time_first"]).reshape(-1), np.float32),
        "tmk": np.ascontiguousarray(np.asarray(inputs["L0_attn_tmk"]).reshape(-1), np.float32),
        "tmv": np.ascontiguousarray(np.asarray(inputs["L0_attn_tmv"]).reshape(-1), np.float32),
        "tmr": np.ascontiguousarray(np.asarray(inputs["L0_attn_tmr"]).reshape(-1), np.float32),
        "ftmk": np.ascontiguousarray(np.asarray(inputs["L0_attn_f_tmk"]).reshape(-1), np.float32),
        "ftmr": np.ascontiguousarray(np.asarray(inputs["L0_attn_f_tmr"]).reshape(-1), np.float32),
    }

    nc = build_nc(T=T, TS=TS, nseq=nseq)
    in_maps = []
    for i in range(NCORES):
        m = dict(common)
        m["x"] = _shape_x(x[i * nseq:(i + 1) * nseq], NTT, TS)
        in_maps.append(m)
    res = run_bass_kernel_spmd(nc, in_maps, core_ids=list(range(NCORES)))
    outs = [_unshape_out(np.asarray(res.results[i]["out"], np.float32), NTT, TS, nseq)
            for i in range(NCORES)]
    return np.ascontiguousarray(np.concatenate(outs, axis=0))


# version 22
# speedup vs baseline: 1.0449x; 1.0449x over previous
import contextlib
import numpy as np

import concourse.bass as bass
import concourse.mybir as mybir
from concourse.bass_utils import run_bass_kernel_spmd

F32 = mybir.dt.float32
BF16 = mybir.dt.bfloat16
I32 = mybir.dt.int32
AF = mybir.ActivationFunctionType
ALU = mybir.AluOpType

P = 128
HI = 8
C = 1024
FHI = 32
GHI = 24
NCORES = 8
EPS = 1e-5
INV127 = np.float32(1.0 / 127.0)


class Ctr:
    def __init__(self, nc, stk, name):
        self.sem = stk.enter_context(nc.semaphore(name))
        self.n = 0


class Plan:
    ENGS = ("sync", "vector", "scalar", "tensor", "gpsimd")

    def __init__(self):
        self.ops = {e: [] for e in self.ENGS}

    def op(self, eng, builder, wait=(), inc=None):
        pairs = tuple((c.sem, v) for (c, v) in wait if v is not None)
        tgt = None
        if inc is not None:
            ctr, k = inc
            ctr.n += k
            tgt = (ctr.sem, k)
        def f(e, pairs=pairs, builder=builder, tgt=tgt):
            for s, v in pairs:
                e.wait_ge(s, v)
            ins = builder(e)
            if tgt is not None:
                ins.then_inc(tgt[0], tgt[1])
        self.ops[eng].append(f)
        return None

    def wait(self, eng, ctr, val):
        self.op(eng, lambda e: None, wait=((ctr, val),))

    def drain(self, eng):
        self.ops[eng].append(lambda e: e.drain())

    def emit(self, nc):
        with nc.Block() as block:
            @block.sync
            def _(e):
                for f in self.ops["sync"]:
                    f(e)
            @block.vector
            def _(e):
                for f in self.ops["vector"]:
                    f(e)
            @block.scalar
            def _(e):
                for f in self.ops["scalar"]:
                    f(e)
            @block.tensor
            def _(e):
                for f in self.ops["tensor"]:
                    f(e)
            @block.gpsimd
            def _(e):
                for f in self.ops["gpsimd"]:
                    f(e)


def build_nc(T=4096, TS=256, nseq=2, debug_scratch=False):
    NTT = T // TS
    NT = nseq * NTT
    FD = HI * TS
    KFD = FHI * TS
    GFD = GHI * TS

    nc = bass.Bass("TRN2", target_bir_lowering=False, debug=False)

    x_in = nc.dram_tensor("x", [NT, P, FD], F32, kind="ExternalInput").ap()
    out_d = nc.dram_tensor("out", [NT, P, FD], F32, kind="ExternalOutput").ap()
    wd_in = {}
    for nm, shp, dt in (("wkT", [C, C], F32), ("wvT", [C, C], F32),
                        ("wrT", [C, C], F32), ("woT", [C, C], F32),
                        ("fwkT", [C, 4096], F32), ("fwrT", [C, C], F32),
                        ("fwvT", [4096, C], F32), ("wgT", [C, 3072], I32),
                        ("wuT", [C, 3072], I32), ("wdT", [3072, C], I32)):
        wd_in[nm] = nc.dram_tensor(nm, shp, dt, kind="ExternalInput").ap()
    vec_in = {}
    for nm in ("ln1_g", "ln1_b", "ln2_g", "ln2_b", "td", "tf",
               "tmk", "tmv", "tmr", "ftmk", "ftmr"):
        vec_in[nm] = nc.dram_tensor(nm, [C], F32, kind="ExternalInput").ap()

    _kind = "ExternalOutput" if debug_scratch else "Internal"
    dbg_d = nc.dram_tensor("dbg", [NT, P, 4 * TS], F32, kind=_kind).ap() \
        if debug_scratch else None
    sry_s = nc.dram_tensor("sry_s", [NT, P, FD], BF16, kind=_kind).ap()
    kk_s = nc.dram_tensor("kk_s", [NT, P, KFD], BF16, kind=_kind).ap()
    r2_s = nc.dram_tensor("r2_s", [NT, P, FD], BF16, kind=_kind).ap()
    s_s = nc.dram_tensor("s_s", [NT, P, FD], F32, kind=_kind).ap()
    x3_s = nc.dram_tensor("x3_s", [NT, P, FD], F32, kind=_kind).ap()

    with contextlib.ExitStack() as g:
        def sbg(name, shape, dt=F32):
            return g.enter_context(nc.sbuf_tensor(name, shape, dt))

        ones = sbg("ones", [P, P], BF16)
        epsb = sbg("epsb", [P, 1])
        vv = {nm: sbg("v_" + nm, [P, HI]) for nm in vec_in}
        wvec = sbg("wvec", [P, HI])
        tmp8 = sbg("tmp8", [P, HI])
        BVk = sbg("BVk", [P, HI * 2], BF16); BVv = sbg("BVv", [P, HI * 2], BF16)
        BVr = sbg("BVr", [P, HI * 2], BF16)
        BFk = sbg("BFk", [P, HI * 2], BF16); BFr = sbg("BFr", [P, HI * 2], BF16)
        bias_k = sbg("bias_k", [P, HI]); ckv_ = sbg("ck", [P, HI])
        bias_v = sbg("bias_v", [P, HI]); cvv_ = sbg("cv", [P, HI])
        bias_r = sbg("bias_r", [P, HI]); crv_ = sbg("cr", [P, HI])
        bias_fk = sbg("bias_fk", [P, FHI]); cfk = sbg("cfk", [P, FHI])
        bias_fr = sbg("bias_fr", [P, HI]); cfr = sbg("cfr", [P, HI])

        PS = g.enter_context(nc.psum_tensor("PS", [P, 4096], F32))

        pl = Plan()
        c_wc = Ctr(nc, g, "c_wc")
        c_pe = Ctr(nc, g, "c_pe")
        c_sc = Ctr(nc, g, "c_sc")
        c_ve = Ctr(nc, g, "c_ve")
        c_bar = Ctr(nc, g, "c_bar")
        c_vec = Ctr(nc, g, "c_vec")
        c_gp = Ctr(nc, g, "c_gp")
        c_stg = Ctr(nc, g, "c_stg")
        c_x = [Ctr(nc, g, "c_x0"), Ctr(nc, g, "c_x1")]
        c_sry = [Ctr(nc, g, "c_sry0"), Ctr(nc, g, "c_sry1")]
        c_b2 = [Ctr(nc, g, "c_b20"), Ctr(nc, g, "c_b21")]
        c_sp_sry = Ctr(nc, g, "c_sp_sry")
        c_sp_kk = Ctr(nc, g, "c_sp_kk")
        c_sp_r2 = Ctr(nc, g, "c_sp_r2")
        c_sp_s = Ctr(nc, g, "c_sp_s")
        c_sp_x3 = Ctr(nc, g, "c_sp_x3")
        c_sp_out = Ctr(nc, g, "c_sp_out")
        DMA_CTRS = [c_vec, c_stg, c_x[0], c_x[1], c_sry[0], c_sry[1],
                    c_b2[0], c_b2[1], c_sp_sry, c_sp_kk, c_sp_r2, c_sp_s,
                    c_sp_x3, c_sp_out]

        def barrier():
            pl.op("vector", lambda e: e.memset(tmp8[:, 0:1], 0.0), inc=(c_bar, 1))
            pl.op("scalar", lambda e: e.activation(out=tmp8[:, 1:2], in_=tmp8[:, 1:2],
                                                   func=AF.Copy), inc=(c_bar, 1))
            pl.op("tensor", lambda e: e.matmul(PS[:, 0:1], ones[:, 0:P],
                                               ones[:, 0:1], start=True, stop=True),
                  wait=((c_sc, c_sc.n), (c_ve, c_ve.n)), inc=(c_bar, 1))
            pl.op("gpsimd", lambda e: e.memset(tmp8[:, 2:3], 0.0), inc=(c_bar, 1))
            tot = c_bar.n
            dma_waits = tuple((c, c.n) for c in DMA_CTRS)
            for eng in Plan.ENGS:
                pl.op(eng, lambda e: None, wait=((c_bar, tot),) + dma_waits)

        def bcast_rows(row):
            return row[:].unsqueeze(1).broadcast_to([P, HI, TS])

        def bcast_cols(col8):
            return col8[:].unsqueeze(2).broadcast_to([P, HI, TS])

        for nm in vec_in:
            pl.op("sync", lambda e, nm=nm: e.dma_start(
                out=vv[nm][:], in_=vec_in[nm].rearrange("(h p) -> p h", p=P),
                allow_slow_non_contiguous=True), inc=(c_vec, 16))
        pl.op("vector", lambda e: e.memset(ones[:], 1.0))
        pl.op("vector", lambda e: e.memset(epsb[:], EPS))
        pl.op("scalar", lambda e: e.activation(out=tmp8[:], in_=vv["td"][:],
                                               func=AF.Exp),
              wait=((c_vec, c_vec.n),), inc=(c_sc, 1))
        pl.op("vector", lambda e: e.tensor_scalar(out=wvec[:], in0=tmp8[:],
                                                  scalar1=-1.0, scalar2=None,
                                                  op0=ALU.mult),
              wait=((c_sc, c_sc.n), (c_vec, c_vec.n)))
        def mk_bv(e):
            for arr, tm, bvec in ((BVk, "tmk", "ln1_b"), (BVv, "tmv", "ln1_b"),
                                  (BVr, "tmr", "ln1_b"), (BFk, "ftmk", "ln2_b"),
                                  (BFr, "ftmr", "ln2_b")):
                av = arr.rearrange("p (h j) -> p h j", j=2)
                e.tensor_copy(out=av[:, :, 0], in_=vv[bvec][:])
                e.drain()
                e.tensor_scalar(out=tmp8[:], in0=vv[tm][:], scalar1=1.0,
                                scalar2=None, op0=ALU.subtract)
                e.drain()
                e.tensor_tensor(out=av[:, :, 1], in0=tmp8[:], in1=vv[bvec][:],
                                op=ALU.mult)
                e.drain()
            e.drain()
            return e.memset(tmp8[:, 0:1], 0.0)
        pl.op("vector", mk_bv, inc=(c_ve, 1))
        vec_ready = c_ve.n

        def stage_weights(alloc, specs):
            stF = alloc("stF", [P, 4096])
            stB = alloc("stB", [P, 4096], BF16)
            acc = alloc("acc", [P, 64 * 2])
            pe_slab = [0]
            for (dram, no, res, gain, kch, barr, noj, bdst, cdst) in specs:
                if barr is not None:
                    pl.op("vector", lambda e: e.memset(acc[:], 0.0),
                          wait=((c_pe, pe_slab[0]),), inc=(c_ve, 1))
                for hi in range(kch):
                    src = dram[hi * P:(hi + 1) * P, :]
                    dst = stF[:, 0:no] if dram.dtype == F32 else \
                        stF.bitcast(dram.dtype)[:, 0:no]
                    pl.op("sync", lambda e, dst=dst, src=src: e.dma_start(
                        out=dst, in_=src),
                        wait=((c_wc, c_wc.n), (c_pe, pe_slab[0])),
                        inc=(c_stg, 16))
                    ldn = c_stg.n
                    srcv = dst
                    if barr is not None:
                        pl.op("vector", lambda e, srcv=srcv, no=no: e.tensor_copy(
                            out=stB[:, 0:no], in_=srcv),
                            wait=((c_stg, ldn), (c_pe, pe_slab[0])), inc=(c_ve, 1))
                        ven = c_ve.n
                        for oj in range(noj):
                            poff = (oj % 8) * 512 + (oj // 8) * 2
                            pl.op("tensor", lambda e, poff=poff, oj=oj, hi=hi,
                                  barr=barr: e.matmul(
                                      PS[:, poff:poff + 2],
                                      stB[:, oj * 128:(oj + 1) * 128],
                                      barr[:, hi * 2:hi * 2 + 2],
                                      start=True, stop=True),
                                  wait=((c_ve, ven), (c_ve, vec_ready))
                                  if oj == 0 else (),
                                  inc=(c_pe, 1) if oj == noj - 1 else None)
                        pe_slab[0] = c_pe.n
                        psv = PS.rearrange("p (a b c) -> p a b c", b=256, c=2)
                        pl.op("vector", lambda e, noj=noj, psv=psv: e.tensor_tensor(
                            out=acc.rearrange("p (a b c) -> p a b c",
                                              a=min(noj, 8), c=2)[:, :, 0:max(noj // 8, 1), :],
                            in0=acc.rearrange("p (a b c) -> p a b c",
                                              a=min(noj, 8), c=2)[:, :, 0:max(noj // 8, 1), :],
                            in1=psv[:, 0:min(noj, 8), 0:max(noj // 8, 1), :],
                            op=ALU.add),
                            wait=((c_pe, pe_slab[0]),), inc=(c_ve, 1))
                        pl.drain("vector")
                        pe_slab[0] = c_pe.n
                    if gain is not None:
                        pl.op("vector", lambda e, res=res, hi=hi, no=no, gain=gain:
                              e.tensor_scalar(out=res[:, hi * no:(hi + 1) * no],
                                              in0=stF[:, 0:no],
                                              scalar1=gain[:, hi:hi + 1],
                                              scalar2=None, op0=ALU.mult),
                              wait=((c_stg, ldn), (c_ve, vec_ready)), inc=(c_wc, 1))
                    else:
                        pl.op("vector", lambda e, res=res, hi=hi, no=no, srcv=srcv:
                              e.tensor_copy(out=res[:, hi * no:(hi + 1) * no],
                                            in_=srcv),
                              wait=((c_stg, ldn),), inc=(c_wc, 1))
                if barr is not None:
                    nb = max(noj // 8, 1)
                    accv = acc.rearrange("p (a b c) -> p a b c", a=min(noj, 8), c=2)
                    def ev(e, bdst=bdst, cdst=cdst, accv=accv, noj=noj, nb=nb):
                        bv = bdst.rearrange("p (b a) -> p a b", a=min(noj, 8))
                        cvx = cdst.rearrange("p (b a) -> p a b", a=min(noj, 8))
                        e.tensor_copy(out=bv[:, :, :], in_=accv[:, :, 0:nb, 0])
                        return e.tensor_copy(out=cvx[:, :, :], in_=accv[:, :, 0:nb, 1])
                    pl.drain("vector")
                    pl.op("vector", ev, inc=(c_ve, 1))
                    pl.drain("vector")

        def _phaseA():
          with contextlib.ExitStack() as ph:
            def sb(name, shape, dt=F32):
                return ph.enter_context(nc.sbuf_tensor("A_" + name, shape, dt))

            wk_res = sb("wk", [P, HI * C], BF16)
            wv_res = sb("wvw", [P, HI * C], BF16)
            wr_res = sb("wr", [P, HI * C], BF16)
            with contextlib.ExitStack() as stg:
                def salloc(n, s, d=F32):
                    return stg.enter_context(nc.sbuf_tensor("As_" + n, s, d))
                stage_weights(salloc, [
                    (wd_in["wkT"], C, wk_res, vv["ln1_g"], HI, BVk, HI, bias_k, ckv_),
                    (wd_in["wvT"], C, wv_res, vv["ln1_g"], HI, BVv, HI, bias_v, cvv_),
                    (wd_in["wrT"], C, wr_res, vv["ln1_g"], HI, BVr, HI, bias_r, crv_)])
            wA_ready = c_ve.n
            wcA = c_wc.n

            xA = [sb("x0", [P, FD]), sb("x1", [P, FD])]
            xbf = sb("xbf", [P, FD], BF16)
            mrow = sb("mrow", [P, TS]); varrow = sb("varrow", [P, TS])
            rstd = sb("rstd", [P, TS])
            tmpA = sb("tmpA", [P, FD])
            zn = sb("zn", [P, HI * (TS + 1)], BF16)
            dmix = sb("dmix", [P, FD], BF16)
            xk = sb("xk", [P, FD], BF16)
            x2bf = xk
            xvm = sb("xvm", [P, FD], BF16)
            xr = sb("xr", [P, FD], BF16)
            PD = TS + 1
            FDP = HI * PD
            kp = [sb("k0", [P, FDP]), sb("k1", [P, FDP])]
            vb = [sb("v0", [P, FD], BF16), sb("v1", [P, FD], BF16)]
            srb = [sb("sr0", [P, FD], BF16), sb("sr1", [P, FD], BF16)]
            PPt = sb("PP", [P, FDP])
            AAt = sb("AA", [P, FDP])
            BBt = sb("BB", [P, FDP])
            e1p = sb("e1p", [P, FDP]); e2p = sb("e2p", [P, FDP])
            Pp = sb("Pp", [P, FDP], BF16)
            Buq = sb("Buq", [P, FD]); Byw = sb("Byw", [P, FD])
            wpad = sb("wpad", [P, FDP])
            sryb = sb("sry", [P, FD], BF16)

            znv = zn.rearrange("p (h t) -> p h t", h=HI)
            PPv = PPt.rearrange("p (h t) -> p h t", h=HI)
            AAv = AAt.rearrange("p (h t) -> p h t", h=HI)
            BBv = BBt.rearrange("p (h t) -> p h t", h=HI)
            e1pv = e1p.rearrange("p (h t) -> p h t", h=HI)
            e2pv = e2p.rearrange("p (h t) -> p h t", h=HI)
            Ppv = Pp.rearrange("p (h t) -> p h t", h=HI)
            wpv = wpad.rearrange("p (h t) -> p h t", h=HI)

            pl.drain("vector")
            for hi in range(HI):
                pl.op("vector", lambda e, hi=hi: e.tensor_copy(
                    out=wpad[:, hi * PD + 1:(hi + 1) * PD],
                    in_=wvec[:, hi:hi + 1].broadcast_to([P, TS])))
            pl.op("vector", lambda e: e.memset(wpv[:, :, 0:1], -1e38))
            pl.op("vector", lambda e: e.memset(e1pv[:, :, 0:1], 0.0))
            pl.op("vector", lambda e: e.memset(e2pv[:, :, 0:1], 0.0))
            pl.drain("vector")
            xAv = [a.rearrange("p (h t) -> p h t", h=HI) for a in xA]
            tmpAv = tmpA.rearrange("p (h t) -> p h t", h=HI)
            dmv = dmix.rearrange("p (h t) -> p h t", h=HI)

            PS_R = 1024
            def st_off(t):
                return (t % 2) * 512
            def k_off(oj):
                return 3072 + oj * TS if oj < 4 else 1024 + (oj - 4) * TS

            sq_d = [0] * (NT + 1); st_d = [0] * (NT + 1)
            vv_d = [0] * (NT + 1); rs_d = [0] * (NT + 1)
            xf_d = [0] * (NT + 1); mx_d = [0] * (NT + 1)
            pjr = [0] * (NT + 1); pjk = [0] * (NT + 1); pjv = [0] * (NT + 1)
            evk1 = [0] * (NT + 1)
            evs = [0] * (NT + 1); evk = [0] * (NT + 1); evv = [0] * (NT + 1)
            vy_d = [0] * (NT + 1); sp_d = [0] * (NT + 1)
            rden_sL = [0] * (NT + 1); num_gL = [0] * (NT + 1)
            d1_vL = [0] * (NT + 1)
            y_gL = [0] * (NT + 1); sry_gL = [0] * (NT + 1)
            x_ld = [0] * (NT + 2)

            def load_x(r, first=False):
                w = ((c_ve, wA_ready), (c_wc, wcA)) if first \
                    else ((c_ve, xf_d[r - 2]),)
                sl = c_x[r % 2]
                pl.op("sync", lambda e, r=r: e.dma_start(out=xA[r % 2][:],
                                                         in_=x_in[r, :, :]),
                      wait=w, inc=(sl, 16))
                x_ld[r] = (sl, sl.n)
            load_x(0, True)
            if NT > 1:
                load_x(1, True)

            for r in range(NT + 2):
                if r >= 2:
                    j2 = r - 2
                    pl.op("gpsimd", lambda e: e.tensor_tensor(
                        out=Buq[:], in0=Buq[:], in1=tmpA[:], op=ALU.mult),
                        wait=((c_sc, rden_sL[j2]),), inc=(c_gp, 1))
                    y_gL[j2] = c_gp.n
                    pl.drain("gpsimd")
                    wsry = []
                    if j2 >= 1:
                        wsry.append((c_sp_sry, sp_d[j2 - 1]))
                    pl.op("gpsimd", lambda e, j2=j2: e.tensor_tensor(
                        out=sryb[:], in0=Buq[:], in1=srb[j2 % 2][:],
                        op=ALU.mult), wait=wsry, inc=(c_gp, 1))
                    sry_gL[j2] = c_gp.n
                    pl.drain("gpsimd")
                    pl.op("gpsimd", lambda e, j2=j2: e.dma_start(
                        out=sry_s[j2, :, :], in_=sryb[:]), inc=(c_sp_sry, 16))
                    sp_d[j2] = c_sp_sry.n
                def emit_castats(t):
                    w = [x_ld[t]]
                    if t >= 1:
                        w.append((c_pe, st_d[t - 1]))
                    pl.op("scalar", lambda e, t=t: e.activation(
                        out=xbf[:], in_=xA[t % 2][:], func=AF.Identity),
                        wait=w)
                    pl.drain("scalar")
                    pl.op("scalar", lambda e: e.activation(
                        out=x2bf[:], in_=xbf[:], func=AF.Square),
                        inc=(c_sc, 1))
                    sq_d[t] = c_sc.n
                    base = st_off(t)
                    wst = [(c_sc, sq_d[t])]
                    if t >= 2:
                        wst.append((c_ve, vv_d[t - 2]))
                    if t == 0:
                        wst += [(c_ve, wA_ready), (c_wc, wcA)]
                    for hi in range(HI):
                        pl.op("tensor", lambda e, hi=hi, base=base: e.matmul(
                            PS[:, base:base + TS], ones[:, :],
                            xbf[:, hi * TS:(hi + 1) * TS], start=(hi == 0),
                            stop=(hi == HI - 1)), wait=wst if hi == 0 else ())
                    for hi in range(HI):
                        pl.op("tensor", lambda e, hi=hi, base=base: e.matmul(
                            PS[:, base + TS:base + 2 * TS], ones[:, :],
                            x2bf[:, hi * TS:(hi + 1) * TS], start=(hi == 0),
                            stop=(hi == HI - 1)),
                            inc=(c_pe, 1) if hi == HI - 1 else None)
                    st_d[t] = c_pe.n
                if r == 0:
                    emit_castats(0)
                if r < NT:
                    sstart = (r % NTT == 0)
                    xcur, xcv = xA[r % 2], xAv[r % 2]
                    pl.op("vector", lambda e, r=r: e.tensor_scalar(
                        out=mrow[:], in0=PS[:, st_off(r):st_off(r) + TS],
                        scalar1=1.0 / C, scalar2=None, op0=ALU.mult),
                        wait=((c_pe, st_d[r]),), inc=(c_ve, 1))
                    mn_v = c_ve.n
                    pl.op("scalar", lambda e: e.activation(
                        out=varrow[:], in_=mrow[:], func=AF.Square),
                        wait=((c_ve, mn_v),), inc=(c_sc, 1))
                    msq_s = c_sc.n
                    pl.op("vector", lambda e, r=r: e.scalar_tensor_tensor(
                        out=varrow[:],
                        in0=PS[:, st_off(r) + TS:st_off(r) + 2 * TS],
                        scalar=1.0 / C, in1=varrow[:], op0=ALU.mult,
                        op1=ALU.subtract), wait=((c_sc, msq_s),),
                        inc=(c_ve, 1))
                    vv_d[r] = c_ve.n
                    pl.op("scalar", lambda e: e.activation(
                        out=rstd[:], in_=varrow[:], func=AF.Ln,
                        bias=epsb[:, 0:1]), wait=((c_ve, vv_d[r]),))
                    pl.drain("scalar")
                    pl.op("scalar", lambda e: e.activation(
                        out=rstd[:], in_=rstd[:], func=AF.Exp, scale=-0.5),
                        inc=(c_sc, 1))
                    rs_d[r] = c_sc.n
                    if r + 1 < NT:
                        emit_castats(r + 1)
                    if sstart:
                        pl.op("vector", lambda e: e.memset(znv[:, :, 0:1], 0.0))
                    else:
                        pl.drain("vector")
                        pl.op("vector", lambda e: e.tensor_copy(
                            out=znv[:, :, 0:1], in_=znv[:, :, TS:TS + 1]))
                    wap = [(c_sc, rs_d[r])]
                    if r >= 2:
                        wap.append((c_gp, y_gL[r - 2]))
                    pl.op("vector", lambda e, xcv=xcv: e.tensor_tensor(
                        out=tmpAv[:, :, :], in0=xcv[:, :, :],
                        in1=bcast_rows(mrow), op=ALU.subtract),
                        wait=wap)
                    pl.drain("vector")
                    pl.op("vector", lambda e: e.tensor_tensor(
                        out=znv[:, :, 1:TS + 1], in0=tmpAv[:, :, :],
                        in1=bcast_rows(rstd), op=ALU.mult), inc=(c_ve, 1))
                    xf_d[r] = c_ve.n
                    pl.drain("vector")
                    def front_b(r=r, sstart=sstart, xcv=xcv):
                        pl.op("vector", lambda e: e.tensor_tensor(
                            out=dmv[:, :, :], in0=znv[:, :, 1:TS + 1],
                            in1=znv[:, :, 0:TS], op=ALU.subtract))
                        pl.drain("vector")
                        for dst, tm in ((xk, "tmk"), (xvm, "tmv"), (xr, "tmr")):
                            for hi in range(HI):
                                first = (dst is xk and hi == 0)
                                last = (dst is xr and hi == HI - 1)
                                wmx = []
                                if first and r + 1 < NT:
                                    wmx.append((c_pe, st_d[r + 1]))
                                if first and r >= 2:
                                    wmx.append((c_gp, num_gL[r - 2]))
                                pl.op("vector", lambda e, dst=dst, tm=tm, hi=hi:
                                      e.scalar_tensor_tensor(
                                          out=dst[:, hi * TS:(hi + 1) * TS],
                                          in0=dmix[:, hi * TS:(hi + 1) * TS],
                                          scalar=vv[tm][:, hi:hi + 1],
                                          in1=znv[:, hi, 0:TS],
                                          op0=ALU.mult, op1=ALU.add),
                                      wait=wmx,
                                      inc=(c_ve, 1) if last else None)
                        mx_d[r] = c_ve.n

                        w = [(c_ve, mx_d[r])]
                        if r >= 1:
                            w.append((c_sc, evk[r - 1]))
                        for oj in range(HI):
                            for hi in range(HI):
                                pl.op("tensor", lambda e, oj=oj, hi=hi: e.matmul(
                                    PS[:, PS_R + oj * TS:PS_R + (oj + 1) * TS],
                                    wr_res[:, hi * C + oj * 128:hi * C + oj * 128 + 128],
                                    xr[:, hi * TS:(hi + 1) * TS],
                                    start=(hi == 0), stop=(hi == HI - 1)),
                                    wait=w if (oj == 0 and hi == 0) else (),
                                    inc=(c_pe, 1) if (oj == HI - 1 and hi == HI - 1)
                                    else None)
                        pjr[r] = c_pe.n
                        vcor = None
                        if sstart:
                            psr = PS[:, PS_R:PS_R + HI * TS].rearrange(
                                "p (o t) -> p o t", o=HI)
                            pl.op("vector", lambda e, psr=psr: e.tensor_tensor(
                                out=psr[:, :, 0:1], in0=psr[:, :, 0:1],
                                in1=crv_[:].unsqueeze(2), op=ALU.add),
                                wait=((c_pe, pjr[r]),), inc=(c_ve, 1))
                            vcor = c_ve.n
                        w = [(c_ve, mx_d[r])]
                        if r >= 1:
                            w.append((c_sc, evk1[r - 1]))
                        for oj in range(4):
                            for hi in range(HI):
                                pl.op("tensor", lambda e, oj=oj, hi=hi: e.matmul(
                                    PS[:, k_off(oj):k_off(oj) + TS],
                                    wk_res[:, hi * C + oj * 128:hi * C + oj * 128 + 128],
                                    xk[:, hi * TS:(hi + 1) * TS],
                                    start=(hi == 0), stop=(hi == HI - 1)),
                                    wait=w if (oj == 0 and hi == 0) else (),
                                    inc=(c_pe, 1) if (oj == 3 and hi == HI - 1)
                                    else None)
                        pjk[r] = c_pe.n
                        return vcor

                if r == 0:
                    vcor = front_b()
                if 1 <= r <= NT:
                    j = r - 1
                    jstart = (j % NTT == 0)
                    kc, vc, src_ = kp[j % 2], vb[j % 2], srb[j % 2]
                    kcv = kc.rearrange("p (h t) -> p h t", h=HI)
                    vcw = vc.rearrange("p (h t) -> p h t", h=HI)
                    Bqv = Buq.rearrange("p (h t) -> p h t", h=HI)
                    Bwv = Byw.rearrange("p (h t) -> p h t", h=HI)
                    if jstart:
                        pl.op("vector", lambda e, kcv=kcv: e.memset(
                            kcv[:, :, 0:1], -1e38), wait=((c_sc, evk[j]),))
                        pl.op("vector", lambda e, kcv=kcv: e.tensor_tensor(
                            out=kcv[:, :, 1:2], in0=kcv[:, :, 1:2],
                            in1=ckv_[:].unsqueeze(2), op=ALU.add))
                        pl.op("vector", lambda e, vcw=vcw: e.tensor_tensor(
                            out=vcw[:, :, 0:1], in0=vcw[:, :, 0:1],
                            in1=cvv_[:].unsqueeze(2), op=ALU.add),
                            wait=((c_sc, evv[j]),))
                    else:
                        pl.drain("vector")
                        pl.op("vector", lambda e, kcv=kcv: e.tensor_copy(
                            out=kcv[:, :, 0:1], in_=PPv[:, :, TS:TS + 1]),
                            wait=((c_sc, evk[j]),))
                    pl.drain("vector")
                    pl.op("vector", lambda e, kc=kc: e.tensor_tensor_scan(
                        out=PPt[:, 0:FDP], data0=wpad[:, 0:FDP],
                        data1=kc[:, 0:FDP], initial=kc[:, 0:1],
                        op0=ALU.add, op1=ALU.max))
                    pl.drain("vector")
                    pl.op("vector", lambda e: e.tensor_tensor(
                        out=e1pv[:, :, 1:PD], in0=PPv[:, :, 0:TS],
                        in1=PPv[:, :, 1:PD], op=ALU.subtract), inc=(c_ve, 1))
                    d_v = c_ve.n
                    for hi in range(HI):
                        pl.op("scalar", lambda e, hi=hi: e.activation(
                            out=e1p[:, hi * PD + 1:(hi + 1) * PD],
                            in_=e1p[:, hi * PD + 1:(hi + 1) * PD], func=AF.Exp,
                            bias=wvec[:, hi:hi + 1]),
                            wait=((c_ve, d_v),) if hi == 0 else (),
                            inc=(c_sc, 1) if hi == HI - 1 else None)
                    e1b_s = c_sc.n
                    pl.op("vector", lambda e, kcv=kcv: e.tensor_tensor(
                        out=e2pv[:, :, 1:PD], in0=kcv[:, :, 1:PD],
                        in1=PPv[:, :, 1:PD], op=ALU.subtract), inc=(c_ve, 1))
                    e2ba_v = c_ve.n
                    pl.op("scalar", lambda e: e.activation(
                        out=e2p[:], in_=e2p[:], func=AF.Exp),
                        wait=((c_ve, e2ba_v),), inc=(c_sc, 1))
                    e2b_s = c_sc.n

                if 1 <= r < NT:
                    vcor = front_b()
                if 1 <= r <= NT:
                    wyw = [(c_gp, sry_gL[r - 2])] if r >= 2 else []
                    pl.op("vector", lambda e, kcv=kcv: e.tensor_tensor(
                        out=Bwv[:, :, :], in0=kcv[:, :, 1:PD],
                        in1=bcast_cols(vv["tf"]), op=ALU.add), wait=wyw)
                    pl.drain("vector")
                    pl.op("vector", lambda e: e.tensor_tensor(
                        out=Bqv[:, :, :], in0=Bwv[:, :, :],
                        in1=PPv[:, :, 0:TS], op=ALU.max))
                    pl.drain("vector")
                    pl.op("vector", lambda e: e.tensor_tensor(
                        out=tmpAv[:, :, :], in0=PPv[:, :, 0:TS],
                        in1=Bqv[:, :, :], op=ALU.subtract), inc=(c_ve, 1))
                    e1a_v = c_ve.n
                    pl.op("scalar", lambda e: e.activation(
                        out=tmpA[:], in_=tmpA[:], func=AF.Exp),
                        wait=((c_ve, e1a_v),), inc=(c_sc, 1))
                    e1_s = c_sc.n
                    pl.op("vector", lambda e: e.tensor_tensor(
                        out=Bwv[:, :, :], in0=Bwv[:, :, :], in1=Bqv[:, :, :],
                        op=ALU.subtract), inc=(c_ve, 1))
                    e2a_v = c_ve.n
                    pl.op("scalar", lambda e: e.activation(
                        out=Byw[:], in_=Byw[:], func=AF.Exp),
                        wait=((c_ve, e2a_v),), inc=(c_sc, 1))
                    e2_s = c_sc.n

                if r < NT:
                    w = [(c_pe, pjr[r]), (c_ve, vcor)]
                    if r >= 2:
                        w.append((c_gp, sry_gL[r - 2]))
                    for oj in range(HI):
                        pl.op("scalar", lambda e, oj=oj, r=r: e.activation(
                            out=srb[r % 2][:, oj * TS:(oj + 1) * TS],
                            in_=PS[:, PS_R + oj * TS:PS_R + (oj + 1) * TS],
                            func=AF.Sigmoid, bias=bias_r[:, oj:oj + 1]),
                            wait=w if oj == 0 else (),
                            inc=(c_sc, 1) if oj == HI - 1 else None)
                    evs[r] = c_sc.n
                    w = [(c_ve, mx_d[r]), (c_sc, evs[r])]
                    for oj in range(HI):
                        for hi in range(HI):
                            pl.op("tensor", lambda e, oj=oj, hi=hi: e.matmul(
                                PS[:, PS_R + oj * TS:PS_R + (oj + 1) * TS],
                                wv_res[:, hi * C + oj * 128:hi * C + oj * 128 + 128],
                                xvm[:, hi * TS:(hi + 1) * TS],
                                start=(hi == 0), stop=(hi == HI - 1)),
                                wait=w if (oj == 0 and hi == 0) else (),
                                inc=(c_pe, 1) if (oj == HI - 1 and hi == HI - 1)
                                else None)
                    pjv[r] = c_pe.n
                    for oj in range(4):
                        pl.op("scalar", lambda e, oj=oj, r=r: e.activation(
                            out=kp[r % 2][:, oj * PD + 1:(oj + 1) * PD],
                            in_=PS[:, k_off(oj):k_off(oj) + TS],
                            func=AF.Identity, bias=bias_k[:, oj:oj + 1]),
                            wait=((c_pe, pjk[r]),) if oj == 0 else (),
                            inc=(c_sc, 1) if oj == 3 else None)
                    evk1[r] = c_sc.n

                if 1 <= r <= NT:
                    j = r - 1
                    if jstart:
                        pl.op("vector", lambda e: e.memset(Ppv[:, :, 0:1], 0.0),
                              wait=((c_sc, e2b_s),))
                        pl.op("vector", lambda e: e.memset(e2pv[:, :, 0:1], 0.0))
                    else:
                        pl.op("vector", lambda e: e.tensor_copy(
                            out=Ppv[:, :, 0:1], in_=AAv[:, :, TS:TS + 1]),
                            wait=((c_sc, e2b_s),))
                        pl.op("vector", lambda e: e.tensor_copy(
                            out=e2pv[:, :, 0:1], in_=BBv[:, :, TS:TS + 1]))
                    pl.op("vector", lambda e, vcw=vcw: e.tensor_tensor(
                        out=Ppv[:, :, 1:PD], in0=e2pv[:, :, 1:PD],
                        in1=vcw[:, :, :], op=ALU.mult),
                        wait=((c_sc, e1b_s), (c_sc, evv[j])))
                    pl.drain("vector")
                    pl.op("vector", lambda e: e.tensor_tensor_scan(
                        out=BBt[:, 0:FDP], data0=e1p[:, 0:FDP],
                        data1=e2p[:, 0:FDP], initial=e2p[:, 0:1],
                        op0=ALU.mult, op1=ALU.add))
                    pl.drain("vector")
                    wd1 = [(c_sc, e1_s)]
                    if j >= 1:
                        wd1.append((c_sp_sry, sp_d[j - 1]))
                    pl.op("vector", lambda e: e.tensor_tensor(
                        out=sryb.rearrange("p (h t) -> p h t", h=HI)[:, :, :],
                        in0=tmpAv[:, :, :], in1=BBv[:, :, 0:TS],
                        op=ALU.mult), wait=wd1, inc=(c_ve, 1))
                    d1_v = c_ve.n
                    d1_vL[j] = d1_v
                    pl.op("vector", lambda e: e.tensor_tensor_scan(
                        out=AAt[:, 0:FDP], data0=e1p[:, 0:FDP],
                        data1=Pp[:, 0:FDP], initial=Pp[:, 0:1],
                        op0=ALU.mult, op1=ALU.add))
                    pl.drain("vector")
                    wn1 = [(c_sc, e1_s)]
                    if r < NT:
                        wn1.append((c_pe, pjv[r]))
                    pl.op("vector", lambda e: e.tensor_tensor(
                        out=xvm.rearrange("p (h t) -> p h t", h=HI)[:, :, :],
                        in0=tmpAv[:, :, :], in1=AAv[:, :, 0:TS],
                        op=ALU.mult), wait=wn1, inc=(c_ve, 1))
                    n1_v = c_ve.n
                    pl.op("gpsimd", lambda e, j=j: e.tensor_tensor(
                        out=Buq[:], in0=Byw[:], in1=vb[j % 2][:], op=ALU.mult),
                        wait=((c_sc, e2_s), (c_ve, e2a_v), (c_sc, evv[j])))
                    pl.drain("gpsimd")
                    pl.op("gpsimd", lambda e: e.tensor_tensor(
                        out=Byw[:], in0=sryb[:], in1=Byw[:], op=ALU.add),
                        wait=((c_ve, d1_v),), inc=(c_gp, 1))
                    den_g = c_gp.n
                    pl.drain("gpsimd")
                    pl.op("gpsimd", lambda e: e.tensor_tensor(
                        out=Buq[:], in0=xvm[:], in1=Buq[:], op=ALU.add),
                        wait=((c_ve, n1_v),), inc=(c_gp, 1))
                    num_gL[j] = c_gp.n
                    pl.drain("gpsimd")

                if r < NT:
                    wve = [(c_pe, pjv[r])]
                    if r >= 2:
                        wve.append((c_gp, num_gL[r - 2]))
                    for oj in range(HI):
                        pl.op("scalar", lambda e, oj=oj, r=r: e.activation(
                            out=vb[r % 2][:, oj * TS:(oj + 1) * TS],
                            in_=PS[:, PS_R + oj * TS:PS_R + (oj + 1) * TS],
                            func=AF.Identity, bias=bias_v[:, oj:oj + 1]),
                            wait=wve if oj == 0 else (),
                            inc=(c_sc, 1) if oj == HI - 1 else None)
                    evv[r] = c_sc.n
                    for oj in range(4, HI):
                        for hi in range(HI):
                            pl.op("tensor", lambda e, oj=oj, hi=hi: e.matmul(
                                PS[:, k_off(oj):k_off(oj) + TS],
                                wk_res[:, hi * C + oj * 128:hi * C + oj * 128 + 128],
                                xk[:, hi * TS:(hi + 1) * TS],
                                start=(hi == 0), stop=(hi == HI - 1)),
                                wait=((c_sc, evv[r]),) if (oj == 4 and hi == 0)
                                else (),
                                inc=(c_pe, 1) if (oj == HI - 1 and hi == HI - 1)
                                else None)
                    pjk2 = c_pe.n
                    for oj in range(4, HI):
                        pl.op("scalar", lambda e, oj=oj, r=r: e.activation(
                            out=kp[r % 2][:, oj * PD + 1:(oj + 1) * PD],
                            in_=PS[:, k_off(oj):k_off(oj) + TS],
                            func=AF.Identity, bias=bias_k[:, oj:oj + 1]),
                            wait=((c_pe, pjk2),) if oj == 4 else (),
                            inc=(c_sc, 1) if oj == HI - 1 else None)
                    evk[r] = c_sc.n
                    if r + 2 < NT:
                        load_x(r + 2)
                if 1 <= r <= NT:
                    j = r - 1
                    pl.op("scalar", lambda e: e.activation(
                        out=tmpA[:], in_=Byw[:], func=AF.Ln),
                        wait=((c_gp, den_g), (c_ve, n1_v)))
                    pl.drain("scalar")
                    pl.op("scalar", lambda e: e.activation(
                        out=tmpA[:], in_=tmpA[:], func=AF.Exp, scale=-1.0),
                        inc=(c_sc, 1))
                    rden_sL[j] = c_sc.n
        _phaseA()
        barrier()

        def _phaseB1():
          with contextlib.ExitStack() as ph:
            def sb(name, shape, dt=F32):
                return ph.enter_context(nc.sbuf_tensor("B_" + name, shape, dt))
            wo_res = sb("wo", [P, HI * C], BF16)
            fwk_res = sb("fwk", [P, HI * 4096], BF16)
            fwr_res = sb("fwr", [P, HI * C], BF16)
            with contextlib.ExitStack() as stg:
                def salloc(n, s, d=F32):
                    return stg.enter_context(nc.sbuf_tensor("Bs_" + n, s, d))
                stage_weights(salloc, [
                    (wd_in["woT"], C, wo_res, None, HI, None, 0, None, None),
                    (wd_in["fwkT"], 4096, fwk_res, vv["ln2_g"], HI, BFk, FHI,
                     bias_fk, cfk),
                    (wd_in["fwrT"], C, fwr_res, vv["ln2_g"], HI, BFr, HI,
                     bias_fr, cfr)])
            wB_ready = c_ve.n
            wcB = c_wc.n

            xB = [sb("x0", [P, FD]), sb("x1", [P, FD])]
            sryB = [sb("sry0", [P, FD], BF16), sb("sry1", [P, FD], BF16)]
            x2B = [sb("x2a", [P, FD]), sb("x2b", [P, FD])]
            xbf = sb("xbf", [P, FD], BF16)
            x2bf = sb("x2bf", [P, FD], BF16)
            mrow = sb("mrow", [P, TS]); varrow = sb("varrow", [P, TS])
            rstd = sb("rstd", [P, TS])
            tmpA = sb("tmpA", [P, FD])
            sbuf_s = tmpA
            zn2 = sb("zn2", [P, HI * (TS + 1)], BF16)
            dmix = sb("dmix", [P, FD], BF16)
            xk2 = [sb("xk2a", [P, FD], BF16), sb("xk2b", [P, FD], BF16)]
            xr2 = [sb("xr2a", [P, FD], BF16), sb("xr2b", [P, FD], BF16)]
            kkb = sb("kk", [P, KFD], BF16)
            r2b = sb("r2", [P, FD], BF16)
            bfk_c = sb("bfkc", [P, FHI])
            bfr_c = sb("bfrc", [P, HI])
            zn2v = zn2.rearrange("p (h t) -> p h t", h=HI)
            x2Bv = [b.rearrange("p (h t) -> p h t", h=HI) for b in x2B]
            tmpAv = tmpA.rearrange("p (h t) -> p h t", h=HI)
            dmv = dmix.rearrange("p (h t) -> p h t", h=HI)

            pl.drain("vector")
            pl.op("vector", lambda e: e.tensor_tensor(
                out=bfk_c[:], in0=bias_fk[:], in1=cfk[:], op=ALU.add),
                wait=((c_ve, wB_ready), (c_wc, wcB)))
            pl.op("vector", lambda e: e.tensor_tensor(
                out=bfr_c[:], in0=bias_fr[:], in1=cfr[:], op=ALU.add),
                inc=(c_ve, 1))
            bc_ready = c_ve.n

            PS_WO = 0
            PS_FK0 = 2048
            PS_ST0 = 3072

            x_ld = [0] * (NT + 3)
            x2e = [0] * (NT + 2); x2f = [0] * (NT + 2); sq2 = [0] * (NT + 2)
            st2 = [0] * (NT + 1); vv2 = [0] * (NT + 1)
            mxk = [0] * (NT + 1); mxr = [0] * (NT + 1)
            wo_pe = [0] * (NT + 2)
            fk_ev = [0] * (NT + 1)
            r2_ev = [0] * (NT + 1)
            vsq_prev = [0, 0, 0, 0]

            def load_b1(r, xw, sw):
                slx, sls = c_x[r % 2], c_sry[r % 2]
                pl.op("sync", lambda e, r=r: e.dma_start(out=xB[r % 2][:],
                                                         in_=x_in[r, :, :]),
                      wait=xw, inc=(slx, 16))
                pl.op("sync", lambda e, r=r: e.dma_start(out=sryB[r % 2][:],
                                                         in_=sry_s[r, :, :]),
                      wait=sw, inc=(sls, 16))
                x_ld[r] = ((slx, slx.n), (sls, sls.n))

            def emit_wo(j):
                w = list(x_ld[j])
                if j >= 1:
                    w.append((c_ve, x2e[j - 1]))
                if j == 0:
                    w += [(c_ve, wB_ready), (c_wc, wcB)]
                for oj in range(HI):
                    for hi in range(HI):
                        pl.op("tensor", lambda e, oj=oj, hi=hi, j=j: e.matmul(
                            PS[:, PS_WO + oj * TS:PS_WO + (oj + 1) * TS],
                            wo_res[:, hi * C + oj * 128:hi * C + oj * 128 + 128],
                            sryB[j % 2][:, hi * TS:(hi + 1) * TS],
                            start=(hi == 0), stop=(hi == HI - 1)),
                            wait=w if (oj == 0 and hi == 0) else (),
                            inc=(c_pe, 1) if (oj == HI - 1 and hi == HI - 1)
                            else None)
                wo_pe[j] = c_pe.n

            def emit_stats(j):
                base = PS_ST0 + (j % 2) * 512
                wst = [(c_ve, max(sq2[j], mxr[j - 1] if j >= 1 else 0))]
                for hi in range(HI):
                    pl.op("tensor", lambda e, hi=hi, base=base: e.matmul(
                        PS[:, base:base + TS], ones[:, :],
                        xbf[:, hi * TS:(hi + 1) * TS], start=(hi == 0),
                        stop=(hi == HI - 1)), wait=wst if hi == 0 else ())
                for hi in range(HI):
                    pl.op("tensor", lambda e, hi=hi, base=base: e.matmul(
                        PS[:, base + TS:base + 2 * TS], ones[:, :],
                        x2bf[:, hi * TS:(hi + 1) * TS], start=(hi == 0),
                        stop=(hi == HI - 1)),
                        inc=(c_pe, 1) if hi == HI - 1 else None)
                st2[j] = c_pe.n

            def emit_x2blk(j):
                pl.drain("vector")
                for h in range(2):
                    pl.op("vector", lambda e, j=j, h=h: e.tensor_tensor(
                        out=x2B[j % 2][:, h * 1024:(h + 1) * 1024],
                        in0=PS[:, PS_WO + h * 1024:PS_WO + (h + 1) * 1024],
                        in1=xB[j % 2][:, h * 1024:(h + 1) * 1024], op=ALU.add),
                        wait=((c_pe, wo_pe[j]),) if h == 0 else (),
                        inc=(c_ve, 1) if h == 1 else None)
                x2e[j] = c_ve.n
                pl.drain("vector")
                w = [(c_sp_s, 16 * j)] if j >= 1 else []
                pl.op("vector", lambda e, j=j: e.tensor_tensor(
                    out=sbuf_s[:], in0=x2B[j % 2][:], in1=xB[j % 2][:],
                    op=ALU.add), wait=w, inc=(c_ve, 1))
                x2f[j] = c_ve.n
                pl.op("gpsimd", lambda e, j=j: e.dma_start(out=s_s[j, :, :],
                                                           in_=sbuf_s[:]),
                      wait=((c_ve, x2f[j]),), inc=(c_sp_s, 16))
                w = [(c_pe, st2[j - 1])] if j >= 1 else []
                pl.op("vector", lambda e, j=j: e.tensor_copy(
                    out=xbf[:], in_=x2B[j % 2][:]), wait=w)
                pl.drain("vector")
                pl.op("vector", lambda e: e.tensor_tensor(
                    out=x2bf[:], in0=xbf[:], in1=xbf[:], op=ALU.mult),
                    inc=(c_ve, 1))
                sq2[j] = c_ve.n

            def emit_front_pre(j):
                base = PS_ST0 + (j % 2) * 512
                pl.op("vector", lambda e, base=base: e.tensor_scalar(
                    out=mrow[:], in0=PS[:, base:base + TS], scalar1=1.0 / C,
                    scalar2=None, op0=ALU.mult), wait=((c_pe, st2[j]),))
                pl.drain("vector")
                pl.op("vector", lambda e: e.tensor_tensor(
                    out=varrow[:], in0=mrow[:], in1=mrow[:], op=ALU.mult))
                pl.drain("vector")
                pl.op("vector", lambda e, base=base: e.scalar_tensor_tensor(
                    out=varrow[:], in0=PS[:, base + TS:base + 2 * TS],
                    scalar=1.0 / C, in1=varrow[:], op0=ALU.mult,
                    op1=ALU.subtract), inc=(c_ve, 1))
                vv2[j] = c_ve.n

            def emit_rstd(j):
                pl.op("scalar", lambda e: e.activation(
                    out=rstd[:], in_=varrow[:], func=AF.Ln, bias=epsb[:, 0:1]),
                    wait=((c_ve, vv2[j]),))
                pl.drain("scalar")
                pl.op("scalar", lambda e: e.activation(
                    out=rstd[:], in_=rstd[:], func=AF.Exp, scale=-0.5),
                    inc=(c_sc, 1))
                return c_sc.n

            def emit_front_post(j, rs2):
                sstart = (j % NTT == 0)
                if sstart:
                    pl.op("vector", lambda e: e.memset(zn2v[:, :, 0:1], 0.0))
                else:
                    pl.drain("vector")
                    pl.op("vector", lambda e: e.tensor_copy(
                        out=zn2v[:, :, 0:1], in_=zn2v[:, :, TS:TS + 1]))
                pl.op("vector", lambda e, j=j: e.tensor_tensor(
                    out=tmpAv[:, :, :], in0=x2Bv[j % 2][:, :, :],
                    in1=bcast_rows(mrow), op=ALU.subtract),
                    wait=((c_sp_s, 16 * (j + 1)),))
                pl.drain("vector")
                pl.op("vector", lambda e: e.tensor_tensor(
                    out=zn2v[:, :, 1:TS + 1], in0=tmpAv[:, :, :],
                    in1=bcast_rows(rstd), op=ALU.mult),
                    wait=((c_sc, rs2),))
                pl.drain("vector")
                pl.op("vector", lambda e: e.tensor_tensor(
                    out=dmv[:, :, :], in0=zn2v[:, :, 1:TS + 1],
                    in1=zn2v[:, :, 0:TS], op=ALU.subtract))
                pl.drain("vector")
                par = j % 2
                for hi in range(HI):
                    pl.op("vector", lambda e, hi=hi, par=par:
                          e.scalar_tensor_tensor(
                        out=xk2[par][:, hi * TS:(hi + 1) * TS],
                        in0=dmix[:, hi * TS:(hi + 1) * TS],
                        scalar=vv["ftmk"][:, hi:hi + 1],
                        in1=zn2v[:, hi, 0:TS], op0=ALU.mult, op1=ALU.add),
                        inc=(c_ve, 1) if hi == HI - 1 else None)
                mxk[j] = c_ve.n
                for hi in range(HI):
                    pl.op("vector", lambda e, hi=hi, par=par:
                          e.scalar_tensor_tensor(
                        out=xr2[par][:, hi * TS:(hi + 1) * TS],
                        in0=dmix[:, hi * TS:(hi + 1) * TS],
                        scalar=vv["ftmr"][:, hi:hi + 1],
                        in1=zn2v[:, hi, 0:TS], op0=ALU.mult, op1=ALU.add),
                        inc=(c_ve, 1) if hi == HI - 1 else None)
                mxr[j] = c_ve.n

            gpe = [0] * 20
            evd = [0] * 20

            def mm_group(g, r):
                par = g % 2
                base = PS_FK0 + par * 512
                is_r = g >= 16
                w = []
                if g == 0:
                    w.append((c_ve, mxk[r]))
                    if r >= 1:
                        w.append((c_sc, r2_ev[r - 1]))
                elif g == 16:
                    w += [(c_ve, mxr[r]), (c_sc, evd[14])]
                elif g == 17:
                    w.append((c_sc, evd[15]))
                elif g >= 2:
                    w.append((c_sc, evd[g - 2]))
                src = xr2[r % 2] if is_r else xk2[r % 2]
                wres = fwr_res if is_r else fwk_res
                wstep = C if is_r else 4096
                for ojl in range(2):
                    oj = (g - 16) * 2 + ojl if is_r else g * 2 + ojl
                    for hi in range(HI):
                        pl.op("tensor", lambda e, oj=oj, ojl=ojl, hi=hi,
                              base=base, src=src, wres=wres, wstep=wstep:
                              e.matmul(
                            PS[:, base + ojl * TS:base + (ojl + 1) * TS],
                            wres[:, hi * wstep + oj * 128:
                                 hi * wstep + oj * 128 + 128],
                            src[:, hi * TS:(hi + 1) * TS],
                            start=(hi == 0), stop=(hi == HI - 1)),
                            wait=w if (ojl == 0 and hi == 0) else (),
                            inc=(c_pe, 1) if (ojl == 1 and hi == HI - 1)
                            else None)
                gpe[g] = c_pe.n
                sstart = (r % NTT == 0)
                outb = r2b if is_r else kkb
                bias = bias_fr if is_r else bias_fk
                biasc = bfr_c if is_r else bfk_c
                fn = AF.Sigmoid if is_r else AF.Relu
                first = True
                for ojl in range(2):
                    oj = (g - 16) * 2 + ojl if is_r else g * 2 + ojl
                    pcol = base + ojl * TS
                    wz = []
                    if first:
                        wz.append((c_pe, gpe[g]))
                        if not is_r and g % 4 == 0 and r >= 1:
                            wz.append((c_ve, vsq_prev[g // 4]))
                        if g == 0 and r >= 1:
                            wz.append((c_sp_kk, 16 * r))
                        if g == 16 and r >= 1:
                            wz.append((c_sp_r2, 16 * r))
                    last = (ojl == 1)
                    if sstart:
                        pl.op("scalar", lambda e, oj=oj, pcol=pcol, outb=outb,
                              biasc=biasc, fn=fn: e.activation(
                            out=outb[:, oj * TS:oj * TS + 1],
                            in_=PS[:, pcol:pcol + 1], func=fn,
                            bias=biasc[:, oj:oj + 1]),
                            wait=wz + [(c_ve, bc_ready)])
                        wz = []
                        pl.op("scalar", lambda e, oj=oj, pcol=pcol, outb=outb,
                              bias=bias, fn=fn: e.activation(
                            out=outb[:, oj * TS + 1:(oj + 1) * TS],
                            in_=PS[:, pcol + 1:pcol + TS], func=fn,
                            bias=bias[:, oj:oj + 1]),
                            inc=(c_sc, 1) if last else None)
                    else:
                        pl.op("scalar", lambda e, oj=oj, pcol=pcol, outb=outb,
                              bias=bias, fn=fn: e.activation(
                            out=outb[:, oj * TS:(oj + 1) * TS],
                            in_=PS[:, pcol:pcol + TS], func=fn,
                            bias=bias[:, oj:oj + 1]),
                            wait=wz,
                            inc=(c_sc, 1) if last else None)
                    first = False
                evd[g] = c_sc.n

            load_b1(0, [(c_ve, wB_ready), (c_wc, wcB)], [(c_sp_sry, 16 * NT)])
            load_b1(1, [(c_ve, wB_ready)], [])
            emit_wo(0)
            emit_x2blk(0)
            emit_stats(0)
            emit_front_pre(0)
            emit_front_post(0, emit_rstd(0))
            emit_wo(1)
            emit_x2blk(1)
            if NT > 2:
                load_b1(2, [(c_ve, x2f[0])], [(c_pe, wo_pe[0])])

            for r in range(NT):
                if r + 2 < NT:
                    emit_wo(r + 2)
                if r + 1 < NT:
                    emit_stats(r + 1)
                for g in range(6):
                    mm_group(g, r)
                if r + 1 < NT:
                    emit_front_pre(r + 1)
                    emit_front_post(r + 1, emit_rstd(r + 1))
                for g in range(6, 20):
                    mm_group(g, r)
                r2_ev[r] = evd[19]
                pl.op("gpsimd", lambda e, r=r: e.dma_start(out=r2_s[r, :, :],
                                                           in_=r2b[:]),
                      wait=((c_sc, r2_ev[r]),), inc=(c_sp_r2, 16))
                if r + 2 < NT:
                    emit_x2blk(r + 2)
                vsq_cur = [0, 0, 0, 0]
                for m in range(4):
                    pl.op("vector", lambda e, m=m: e.tensor_tensor(
                        out=kkb[:, m * 2048:(m + 1) * 2048],
                        in0=kkb[:, m * 2048:(m + 1) * 2048],
                        in1=kkb[:, m * 2048:(m + 1) * 2048], op=ALU.mult),
                        wait=((c_sc, evd[4 * m + 3]),), inc=(c_ve, 1))
                    vsq_cur[m] = c_ve.n
                vsq_prev = vsq_cur
                fk_ev[r] = vsq_cur[3]
                pl.op("gpsimd", lambda e, r=r: e.dma_start(out=kk_s[r, :, :],
                                                           in_=kkb[:]),
                      wait=((c_ve, fk_ev[r]),), inc=(c_sp_kk, 16))
                if r + 3 < NT:
                    load_b1(r + 3, [(c_ve, x2f[r + 1])], [(c_pe, wo_pe[r + 1])])
        _phaseB1()
        barrier()

        def _phaseB2():
          with contextlib.ExitStack() as ph:
            def sb(name, shape, dt=F32):
                return ph.enter_context(nc.sbuf_tensor("C_" + name, shape, dt))
            fwv_res = sb("fwv", [P, FHI * C], BF16)
            with contextlib.ExitStack() as stg:
                def salloc(n, s, d=F32):
                    return stg.enter_context(nc.sbuf_tensor("Cs_" + n, s, d))
                stage_weights(salloc, [
                    (wd_in["fwvT"], C, fwv_res, None, FHI, None, 0, None, None)])
            wcC = c_wc.n

            kkB = [sb("kk0", [P, KFD], BF16), sb("kk1", [P, KFD], BF16)]
            r2B = [sb("r20", [P, FD], BF16), sb("r21", [P, FD], BF16)]
            sB = [sb("s0", [P, FD]), sb("s1", [P, FD])]
            out1B = [sb("o10", [P, FD]), sb("o11", [P, FD])]
            xbfB = [sb("xbf0", [P, FD], BF16), sb("xbf1", [P, FD], BF16)]
            x2bfB = [sb("x2bf0", [P, FD], BF16), sb("x2bf1", [P, FD], BF16)]
            mrow = sb("mrow", [P, TS]); varrow = sb("varrow", [P, TS])
            rstd = sb("rstd", [P, TS])
            tmpA = sb("tmpA", [P, FD])
            x3buf = sb("x3", [P, FD])
            o1Bv = [b.rearrange("p (h t) -> p h t", h=HI) for b in out1B]
            tmpAv = tmpA.rearrange("p (h t) -> p h t", h=HI)
            x3v = x3buf.rearrange("p (h t) -> p h t", h=HI)

            PS_RKV0 = 0
            PS_ST2 = 2048
            ld2 = [0] * (NT + 2)
            use2 = [0] * (NT + 1)
            sq3 = [0] * (NT + 1); st3 = [0] * (NT + 1)
            vv3 = [0] * (NT + 1)
            sp3 = [0] * (NT + 1)
            evd2 = [0, 0, 0, 0]
            rkv_g = [0, 0, 0, 0]

            def load_b2(r, first=False):
                w = ((c_wc, wcC),) if first else ((c_ve, use2[r - 2]),)
                sl = c_b2[r % 2]
                pl.op("sync", lambda e, r=r: e.dma_start(out=kkB[r % 2][:],
                                                         in_=kk_s[r, :, :]),
                      wait=w, inc=(sl, 16))
                pl.op("sync", lambda e, r=r: e.dma_start(out=r2B[r % 2][:],
                                                         in_=r2_s[r, :, :]),
                      inc=(sl, 16))
                pl.op("sync", lambda e, r=r: e.dma_start(out=sB[r % 2][:],
                                                         in_=s_s[r, :, :]),
                      inc=(sl, 16))
                ld2[r] = (sl, sl.n)

            def emit_stats3(j):
                base = PS_ST2 + (j % 2) * 512
                wst = [(c_ve, max(sq3[j], vv3[j - 2] if j >= 2 else 0))]
                for hi in range(HI):
                    pl.op("tensor", lambda e, hi=hi, base=base, j=j: e.matmul(
                        PS[:, base:base + TS], ones[:, :],
                        xbfB[j % 2][:, hi * TS:(hi + 1) * TS], start=(hi == 0),
                        stop=(hi == HI - 1)), wait=wst if hi == 0 else ())
                for hi in range(HI):
                    pl.op("tensor", lambda e, hi=hi, base=base, j=j: e.matmul(
                        PS[:, base + TS:base + 2 * TS], ones[:, :],
                        x2bfB[j % 2][:, hi * TS:(hi + 1) * TS], start=(hi == 0),
                        stop=(hi == HI - 1)),
                        inc=(c_pe, 1) if hi == HI - 1 else None)
                st3[j] = c_pe.n

            def emit_front3(j):
                base = PS_ST2 + (j % 2) * 512
                pl.drain("vector")
                pl.op("vector", lambda e, base=base: e.tensor_scalar(
                    out=mrow[:], in0=PS[:, base:base + TS], scalar1=1.0 / C,
                    scalar2=None, op0=ALU.mult), wait=((c_pe, st3[j]),))
                pl.drain("vector")
                pl.op("vector", lambda e: e.tensor_tensor(
                    out=varrow[:], in0=mrow[:], in1=mrow[:], op=ALU.mult))
                pl.drain("vector")
                pl.op("vector", lambda e, base=base: e.scalar_tensor_tensor(
                    out=varrow[:], in0=PS[:, base + TS:base + 2 * TS],
                    scalar=1.0 / C, in1=varrow[:], op0=ALU.mult,
                    op1=ALU.subtract), inc=(c_ve, 1))
                vv3[j] = c_ve.n
                pl.op("scalar", lambda e: e.activation(
                    out=rstd[:], in_=varrow[:], func=AF.Ln, bias=epsb[:, 0:1]),
                    wait=((c_ve, vv3[j]),))
                pl.drain("scalar")
                pl.op("scalar", lambda e: e.activation(
                    out=rstd[:], in_=rstd[:], func=AF.Exp, scale=-0.5),
                    inc=(c_sc, 1))
                rsv = c_sc.n
                w = [(c_sc, rsv)]
                if j >= 1:
                    w.append((c_sp_x3, sp3[j - 1]))
                pl.op("vector", lambda e, j=j: e.tensor_tensor(
                    out=tmpAv[:, :, :], in0=o1Bv[j % 2][:, :, :],
                    in1=bcast_rows(mrow), op=ALU.subtract), wait=w)
                pl.drain("vector")
                pl.op("vector", lambda e: e.tensor_tensor(
                    out=x3v[:, :, :], in0=tmpAv[:, :, :], in1=bcast_rows(rstd),
                    op=ALU.mult), inc=(c_ve, 1))
                pl.op("gpsimd", lambda e, j=j: e.dma_start(out=x3_s[j, :, :],
                                                           in_=x3buf[:]),
                      wait=((c_ve, c_ve.n),), inc=(c_sp_x3, 16))
                sp3[j] = c_sp_x3.n

            load_b2(0, True)
            if NT > 1:
                load_b2(1, True)

            for r in range(NT):
                pl.drain("vector")
                for g in range(4):
                    base = PS_RKV0 + (g % 2) * 512
                    w = []
                    if g == 0:
                        w.append(ld2[r])
                        if r >= 1:
                            w.append((c_ve, evd2[2]))
                    elif g == 1:
                        if r >= 1:
                            w.append((c_ve, evd2[3]))
                    else:
                        w.append((c_ve, evd2[g - 2]))
                    for ojl in range(2):
                        oj = g * 2 + ojl
                        for khi in range(FHI):
                            pl.op("tensor", lambda e, oj=oj, ojl=ojl, khi=khi,
                                  r=r, base=base: e.matmul(
                                PS[:, base + ojl * TS:base + (ojl + 1) * TS],
                                fwv_res[:, khi * C + oj * 128:
                                        khi * C + oj * 128 + 128],
                                kkB[r % 2][:, khi * TS:(khi + 1) * TS],
                                start=(khi == 0), stop=(khi == FHI - 1)),
                                wait=w if (ojl == 0 and khi == 0) else (),
                                inc=(c_pe, 1) if (ojl == 1 and khi == FHI - 1)
                                else None)
                    rkv_g[g] = c_pe.n
                    pl.op("vector", lambda e, g=g, r=r, base=base:
                          e.tensor_tensor(
                        out=tmpA[:, g * 512:(g + 1) * 512],
                        in0=PS[:, base:base + 512],
                        in1=r2B[r % 2][:, g * 512:(g + 1) * 512],
                        op=ALU.mult),
                        wait=((c_pe, rkv_g[g]),), inc=(c_ve, 1))
                    evd2[g] = c_ve.n
                pl.drain("vector")
                pl.op("vector", lambda e, r=r: e.tensor_tensor(
                    out=out1B[r % 2][:], in0=tmpA[:], in1=sB[r % 2][:],
                    op=ALU.add), inc=(c_ve, 1))
                use2[r] = c_ve.n
                if r + 2 < NT:
                    load_b2(r + 2)
                pl.drain("vector")
                pl.op("vector", lambda e, r=r: e.tensor_copy(
                    out=xbfB[r % 2][:], in_=out1B[r % 2][:]),
                    wait=[(c_pe, st3[r - 2])] if r >= 2 else [])
                pl.drain("vector")
                pl.op("vector", lambda e, r=r: e.tensor_tensor(
                    out=x2bfB[r % 2][:], in0=xbfB[r % 2][:],
                    in1=xbfB[r % 2][:], op=ALU.mult), inc=(c_ve, 1))
                sq3[r] = c_ve.n
                if r >= 1:
                    emit_stats3(r - 1)
                    emit_front3(r - 1)
            emit_stats3(NT - 1)
            emit_front3(NT - 1)
        _phaseB2()
        barrier()

        def _phaseC():
          with contextlib.ExitStack() as ph:
            def sb(name, shape, dt=F32):
                return ph.enter_context(nc.sbuf_tensor("D_" + name, shape, dt))
            wg_res = sb("wg", [P, HI * 3072], BF16)
            wu_res = sb("wu", [P, HI * 3072], BF16)
            wd_res = sb("wd", [P, GHI * C], BF16)
            with contextlib.ExitStack() as stg:
                def salloc(n, s, d=F32):
                    return stg.enter_context(nc.sbuf_tensor("Ds_" + n, s, d))
                stage_weights(salloc, [
                    (wd_in["wgT"], 3072, wg_res, None, HI, None, 0, None, None),
                    (wd_in["wuT"], 3072, wu_res, None, HI, None, 0, None, None),
                    (wd_in["wdT"], C, wd_res, None, GHI, None, 0, None, None)])
            wcD = c_wc.n

            x3C = [sb("x30", [P, FD]), sb("x31", [P, FD])]
            x3bf = sb("x3bf", [P, FD], BF16)
            hbuf = sb("h", [P, GFD], BF16)
            gtmp = sb("gt", [P, TS])
            lnin = sb("lnin", [P, FD])
            xbf = sb("xbf", [P, FD], BF16)
            x2bf = sb("x2bf", [P, FD], BF16)
            mrow = sb("mrow", [P, TS]); varrow = sb("varrow", [P, TS])
            rstd = sb("rstd", [P, TS])
            tmpA = sb("tmpA", [P, FD])
            lnv = lnin.rearrange("p (h t) -> p h t", h=HI)
            tmpAv = tmpA.rearrange("p (h t) -> p h t", h=HI)

            PS_G = 2048
            PS_U = 3072
            PS_D = 0
            PS_ST3 = 2048

            ld3 = [0] * (NT + 2)
            use3 = [0] * (NT + 1)
            sq4 = [0] * (NT + 1); st4 = [0] * (NT + 1)
            mn4 = [0] * (NT + 1); vv4 = [0] * (NT + 1)
            dn_d = [0] * (NT + 1)
            out_sp = [0] * (NT + 1)
            h_ev = {}

            def load_c(r, first=False):
                w = ((c_wc, wcD),) if first else ((c_ve, use3[r - 2]),)
                sl = c_x[r % 2]
                pl.op("sync", lambda e, r=r: e.dma_start(out=x3C[r % 2][:],
                                                         in_=x3_s[r, :, :]),
                      wait=w, inc=(sl, 16))
                ld3[r] = (sl, sl.n)
            load_c(0, True)
            if NT > 1:
                load_c(1, True)

            cast_s = [0] * (NT + 1)

            def emit_cast3(t, wextra=()):
                w = [ld3[t]] + list(wextra)
                pl.op("scalar", lambda e, t=t: e.activation(
                    out=x3bf[:], in_=x3C[t % 2][:], func=AF.Identity),
                    wait=w, inc=(c_sc, 1))
                cast_s[t] = c_sc.n

            emit_cast3(0)
            for r in range(NT + 1):
                if r < NT:
                    for oj in range(GHI):
                        par = oj % 2
                        wg0 = [(c_sc, cast_s[r])]
                        if oj >= 2:
                            wg0.append((c_ve, h_ev[(r, oj - 2)]))
                        elif r >= 1:
                            wg0.append((c_ve, h_ev[(r - 1, GHI - 2 + oj)]))
                            if oj == 0 and r >= 2:
                                wg0 += [(c_sc, mn4[r - 2]), (c_ve, vv4[r - 2])]
                        for hi in range(HI):
                            pl.op("tensor", lambda e, oj=oj, hi=hi, par=par: e.matmul(
                                PS[:, PS_G + par * 512:PS_G + par * 512 + TS],
                                wg_res[:, hi * 3072 + oj * 128:hi * 3072 + oj * 128 + 128],
                                x3bf[:, hi * TS:(hi + 1) * TS],
                                start=(hi == 0), stop=(hi == HI - 1)),
                                wait=wg0 if hi == 0 else ())
                        for hi in range(HI):
                            pl.op("tensor", lambda e, oj=oj, hi=hi, par=par: e.matmul(
                                PS[:, PS_U + par * 512:PS_U + par * 512 + TS],
                                wu_res[:, hi * 3072 + oj * 128:hi * 3072 + oj * 128 + 128],
                                x3bf[:, hi * TS:(hi + 1) * TS],
                                start=(hi == 0), stop=(hi == HI - 1)),
                                inc=(c_pe, 1) if hi == HI - 1 else None)
                        gu_pe = c_pe.n
                        wsil = [(c_pe, gu_pe)]
                        if oj >= 1:
                            wsil.append((c_ve, h_ev[(r, oj - 1)]))
                        pl.op("scalar", lambda e, par=par: e.activation(
                            out=gtmp[:], in_=PS[:, PS_G + par * 512:PS_G + par * 512 + TS],
                            func=AF.Silu, scale=float(INV127)),
                            wait=wsil, inc=(c_sc, 1))
                        sild = c_sc.n
                        pl.op("vector", lambda e, oj=oj, par=par: e.tensor_tensor(
                            out=hbuf[:, oj * TS:(oj + 1) * TS], in0=gtmp[:],
                            in1=PS[:, PS_U + par * 512:PS_U + par * 512 + TS],
                            op=ALU.mult),
                            wait=((c_sc, sild),), inc=(c_ve, 1))
                        h_ev[(r, oj)] = c_ve.n
                    if r + 1 < NT:
                        emit_cast3(r + 1, wextra=((c_pe, gu_pe),))
                    w = [(c_ve, h_ev[(r, GHI - 1)])]
                    if r >= 1:
                        w.append((c_ve, use3[r - 1]))
                    for oj in range(HI):
                        for khi in range(GHI):
                            pl.op("tensor", lambda e, oj=oj, khi=khi: e.matmul(
                                PS[:, PS_D + oj * TS:PS_D + (oj + 1) * TS],
                                wd_res[:, khi * C + oj * 128:khi * C + oj * 128 + 128],
                                hbuf[:, khi * TS:(khi + 1) * TS],
                                start=(khi == 0), stop=(khi == GHI - 1)),
                                wait=w if (oj == 0 and khi == 0) else (),
                                inc=(c_pe, 1) if (oj == HI - 1 and khi == GHI - 1)
                                else None)
                    dn_d[r] = c_pe.n

                if r >= 1:
                    j = r - 1
                    pl.drain("vector")
                    w = []
                    if j >= 1:
                        w += [(c_sc, sq4[j - 1]), (c_pe, st4[j - 1])]
                    pl.op("vector", lambda e: e.tensor_copy(out=xbf[:], in_=lnin[:]),
                          wait=w, inc=(c_ve, 1))
                    castv2 = c_ve.n
                    pl.op("scalar", lambda e: e.activation(out=x2bf[:], in_=xbf[:],
                                                           func=AF.Square),
                          wait=((c_ve, castv2),), inc=(c_sc, 1))
                    sq4[j] = c_sc.n
                    wst = [(c_sc, sq4[j])]
                    if j >= 1:
                        wst += [(c_ve, vv4[j - 1]), (c_sc, mn4[j - 1])]
                    for hi in range(HI):
                        pl.op("tensor", lambda e, hi=hi: e.matmul(
                            PS[:, PS_ST3:PS_ST3 + TS], ones[:, :],
                            xbf[:, hi * TS:(hi + 1) * TS], start=(hi == 0),
                            stop=(hi == HI - 1)), wait=wst if hi == 0 else ())
                    for hi in range(HI):
                        pl.op("tensor", lambda e, hi=hi: e.matmul(
                            PS[:, PS_ST3 + TS:PS_ST3 + 2 * TS], ones[:, :],
                            x2bf[:, hi * TS:(hi + 1) * TS], start=(hi == 0),
                            stop=(hi == HI - 1)),
                            inc=(c_pe, 1) if hi == HI - 1 else None)
                    st4[j] = c_pe.n
                    pl.op("scalar", lambda e: e.activation(
                        out=mrow[:], in_=PS[:, PS_ST3:PS_ST3 + TS], func=AF.Copy,
                        scale=1.0 / C), wait=((c_pe, st4[j]),), inc=(c_sc, 1))
                    mn4[j] = c_sc.n
                    pl.op("vector", lambda e: e.tensor_tensor(
                        out=varrow[:], in0=mrow[:], in1=mrow[:], op=ALU.mult),
                        wait=((c_sc, mn4[j]),))
                    pl.drain("vector")
                    pl.op("vector", lambda e: e.scalar_tensor_tensor(
                        out=varrow[:], in0=PS[:, PS_ST3 + TS:PS_ST3 + 2 * TS],
                        scalar=1.0 / C, in1=varrow[:], op0=ALU.mult,
                        op1=ALU.subtract), inc=(c_ve, 1))
                    vv4[j] = c_ve.n
                    pl.op("scalar", lambda e: e.activation(
                        out=rstd[:], in_=varrow[:], func=AF.Ln, bias=epsb[:, 0:1]),
                        wait=((c_ve, vv4[j]),))
                    pl.drain("scalar")
                    pl.op("scalar", lambda e: e.activation(
                        out=rstd[:], in_=rstd[:], func=AF.Exp, scale=-0.5),
                        inc=(c_sc, 1))
                    rsv = c_sc.n
                    w = [(c_sc, rsv)]
                    if j >= 1:
                        w.append((c_sp_out, out_sp[j - 1]))
                    pl.op("vector", lambda e: e.tensor_tensor(
                        out=tmpAv[:, :, :], in0=lnv[:, :, :], in1=bcast_rows(mrow),
                        op=ALU.subtract), wait=w)
                    pl.drain("vector")
                    pl.op("vector", lambda e: e.tensor_tensor(
                        out=tmpAv[:, :, :], in0=tmpAv[:, :, :], in1=bcast_rows(rstd),
                        op=ALU.mult), inc=(c_ve, 1))
                    pl.op("gpsimd", lambda e, j=j: e.dma_start(out=out_d[j, :, :],
                                                               in_=tmpA[:]),
                          wait=((c_ve, c_ve.n),), inc=(c_sp_out, 16))
                    out_sp[j] = c_sp_out.n

                if r < NT:
                    pl.drain("vector")
                    for oj in range(HI):
                        pl.op("vector", lambda e, oj=oj, r=r: e.scalar_tensor_tensor(
                            out=lnin[:, oj * TS:(oj + 1) * TS],
                            in0=PS[:, PS_D + oj * TS:PS_D + (oj + 1) * TS],
                            scalar=float(INV127 * INV127),
                            in1=x3C[r % 2][:, oj * TS:(oj + 1) * TS],
                            op0=ALU.mult, op1=ALU.add),
                            wait=((c_pe, dn_d[r]),) if oj == 0 else (),
                            inc=(c_ve, 1) if oj == HI - 1 else None)
                    use3[r] = c_ve.n
                    if r + 2 < NT:
                        load_c(r + 2)
            pl.wait("gpsimd", c_sp_out, c_sp_out.n)
        _phaseC()

        pl.emit(nc)
    return nc


def _shape_x(x_core, NTT, TS):
    nseq, T, Cc = x_core.shape
    r = x_core.reshape(nseq, NTT, TS, HI, P).transpose(0, 1, 4, 3, 2)
    return np.ascontiguousarray(r.reshape(nseq * NTT, P, HI * TS), dtype=np.float32)


def _unshape_out(o, NTT, TS, nseq):
    r = o.reshape(nseq, NTT, P, HI, TS).transpose(0, 1, 4, 3, 2)
    return r.reshape(nseq, NTT * TS, HI * P)


def kernel(**inputs):
    T, TS = 4096, 256
    NTT = T // TS
    x = np.ascontiguousarray(np.asarray(inputs["L0_attn_x"]), dtype=np.float32)
    B = x.shape[0]
    nseq = B // NCORES

    def tr(a, dt):
        return np.ascontiguousarray(np.asarray(a).T, dt)

    common = {
        "wkT": tr(inputs["L0_attn_Wk"], np.float32),
        "wvT": tr(inputs["L0_attn_Wv"], np.float32),
        "wrT": tr(inputs["L0_attn_Wr"], np.float32),
        "woT": tr(inputs["L0_attn_Wo"], np.float32),
        "fwkT": tr(inputs["L0_attn_fWk"], np.float32),
        "fwrT": tr(inputs["L0_attn_fWr"], np.float32),
        "fwvT": tr(inputs["L0_attn_fWv"], np.float32),
        "wgT": tr(inputs["L0_ffn_w_gate"], np.int32),
        "wuT": tr(inputs["L0_ffn_w_up"], np.int32),
        "wdT": tr(inputs["L0_ffn_w_down"], np.int32),
        "ln1_g": np.ascontiguousarray(np.asarray(inputs["L0_attn_ln1_g"]).reshape(-1), np.float32),
        "ln1_b": np.ascontiguousarray(np.asarray(inputs["L0_attn_ln1_b"]).reshape(-1), np.float32),
        "ln2_g": np.ascontiguousarray(np.asarray(inputs["L0_attn_ln2_g"]).reshape(-1), np.float32),
        "ln2_b": np.ascontiguousarray(np.asarray(inputs["L0_attn_ln2_b"]).reshape(-1), np.float32),
        "td": np.ascontiguousarray(np.asarray(inputs["L0_attn_time_decay"]).reshape(-1), np.float32),
        "tf": np.ascontiguousarray(np.asarray(inputs["L0_attn_time_first"]).reshape(-1), np.float32),
        "tmk": np.ascontiguousarray(np.asarray(inputs["L0_attn_tmk"]).reshape(-1), np.float32),
        "tmv": np.ascontiguousarray(np.asarray(inputs["L0_attn_tmv"]).reshape(-1), np.float32),
        "tmr": np.ascontiguousarray(np.asarray(inputs["L0_attn_tmr"]).reshape(-1), np.float32),
        "ftmk": np.ascontiguousarray(np.asarray(inputs["L0_attn_f_tmk"]).reshape(-1), np.float32),
        "ftmr": np.ascontiguousarray(np.asarray(inputs["L0_attn_f_tmr"]).reshape(-1), np.float32),
    }

    nc = build_nc(T=T, TS=TS, nseq=nseq)
    in_maps = []
    for i in range(NCORES):
        m = dict(common)
        m["x"] = _shape_x(x[i * nseq:(i + 1) * nseq], NTT, TS)
        in_maps.append(m)
    res = run_bass_kernel_spmd(nc, in_maps, core_ids=list(range(NCORES)))
    outs = [_unshape_out(np.asarray(res.results[i]["out"], np.float32), NTT, TS, nseq)
            for i in range(NCORES)]
    return np.ascontiguousarray(np.concatenate(outs, axis=0))
